# version 04
# baseline (speedup 1.0000x reference)
import contextlib
import math
import numpy as np

import concourse.bass as bass
import concourse.mybir as mybir
from concourse.bass_utils import run_bass_kernel_spmd

F32 = mybir.dt.float32
BF16 = mybir.dt.bfloat16
U16 = mybir.dt.uint16
AL = mybir.AluOpType
AF = mybir.ActivationFunctionType
AX = mybir.AxisListType.X

T = 2048
NEG = -1.0e9
PI = math.pi

ENGS = ("sync", "scalar", "vector", "tensor", "gpsimd")
ESEM = {"sync": "dS", "scalar": "S", "vector": "V", "tensor": "T", "gpsimd": "G"}
DSEM = {"sync": "dS", "scalar": "dC", "gpsimd": "dG"}


class Builder:
    def __init__(self, nc, ctx):
        self.nc = nc
        self.q = {e: [] for e in ENGS}
        self.sem = {}
        self.cnt = {}
        for name in ("T", "V", "S", "G", "dS", "dC", "dG"):
            self.sem[name] = ctx.enter_context(nc.semaphore("sem_" + name))
            self.cnt[name] = 0
        self.relaxed = None

    def wall(self):
        return tuple((n, v) for n, v in self.cnt.items() if v > 0)

    @contextlib.contextmanager
    def relax(self):
        prev = self.relaxed
        self.relaxed = self.wall()
        try:
            yield
        finally:
            self.relaxed = prev

    def emit(self, eng, fn, waits=(), inc=None, inc_k=1):
        if self.relaxed is None:
            auto = tuple((n, v) for n, v in self.cnt.items() if v > 0)
        else:
            auto = tuple(self.relaxed)
        w = auto + tuple(waits)
        if inc is not None:
            self.cnt[inc] += inc_k
        self.q[eng].append((w, fn, inc, inc_k))

    def snap(self, *names):
        if len(names) == 1:
            return self.cnt[names[0]]
        return tuple(self.cnt[n] for n in names)

    def dma(self, eng, out, in_, waits=(), **kw):
        sem_name = DSEM[eng]
        self.emit(eng, lambda e, o=out, i=in_, k=dict(kw): e.dma_start(out=o, in_=i, **k),
                  waits=waits, inc=sem_name, inc_k=16)
        return self.cnt[sem_name]

    def replay(self, block):
        def make_runner(engname):
            items = self.q[engname]

            def run(eng):
                wd = {}
                for (w, fn, inc, k) in items:
                    for (s, v) in w:
                        if wd.get(s, -1) < v:
                            eng.wait_ge(self.sem[s], v)
                            wd[s] = v
                    if fn is None:
                        continue
                    inst = fn(eng)
                    if inc is not None:
                        inst.then_inc(self.sem[inc], k)
            return run

        block.sync(make_runner("sync"))
        block.scalar(make_runner("scalar"))
        block.vector(make_runner("vector"))
        block.tensor(make_runner("tensor"))
        block.gpsimd(make_runner("gpsimd"))


class _Done(Exception):
    pass


def build_program(taps=(), upto=10 ** 9):
    nc = bass.Bass("TRN2", target_bir_lowering=False)
    dram = {}

    def din(name, shape, dtype=F32):
        dram[name] = nc.dram_tensor(name, shape, dtype, kind="ExternalInput").ap()
        return dram[name]

    def dscr(name, shape, dtype=F32):
        kind = "ExternalOutput" if name in taps else "Internal"
        dram[name] = nc.dram_tensor(name, shape, dtype, kind=kind).ap()
        return dram[name]

    x_fm = din("x_fm", [8, 128, 2048])
    wq_d = din("wq_d", [128, 8, 256])
    wqc_d = din("wqc_d", [128, 2, 896])
    wqr_d = din("wqr_d", [128, 8, 128])
    wqr_r = din("wqr_r", [128, 8, 128])
    wkv_d = din("wkv_d", [128, 8, 256])
    wkr_d = din("wkr_d", [128, 8, 128])
    wkr_r = din("wkr_r", [128, 8, 128])
    wkc_d = din("wkc_d", [128, 2, 896])
    wv_d = din("wv_d", [128, 2, 1024])
    wo_d = din("wo_d", [128, 8, 1024])
    biases_d = din("biases_d", [128, 120])
    rows_d = din("rows_d", [8, 1024])
    k0_d = din("k0_d", [128, 8, 1024])
    a_d = din("a_d", [128, 128])
    b_mat = din("b_mat", [16, 1024])
    wq1_d = din("wq1_d", [128, 8, 8, 128])
    wk1_d = din("wk1_d", [128, 8, 8, 128])
    wv1_d = din("wv1_d", [128, 8, 1024])
    wc2q_d = din("wc2q_d", [128, 8, 8])
    wo1_d = din("wo1_d", [128, 8, 1024])
    w1t_d = din("w1t_d", [128, 32, 1024])
    wm_d = din("wm_d", [128, 32, 8])
    wqot_d = din("wqot_d", [128, 32, 8])
    bqo_d = din("bqo_d", [128, 32])
    b1_d = din("b1_d", [128, 32])
    w2_d = din("w2_d", [128, 32, 1024])
    rw_d = din("rw_d", [128, 8, 8])
    w1m_d = din("w1m_d", [8, 128, 8, 32, 256])
    w2m_d = din("w2m_d", [8, 128, 32, 1024])
    identf = din("identf", [128, 128])
    cossin_d = din("cossin_d", [128, 4096])
    masks_d = din("masks_d", [128, 4, 512], BF16)
    invvals_d = din("invvals_d", [128, 16, 3], BF16)
    bcfree_d = din("bcfree_d", [128, 512])
    tri128_d = din("tri128_d", [128, 128], BF16)

    g0_sp = dscr("g0_sp", [4, 128, 4096])
    q0h_sp = dscr("q0h_sp", [4, 128, 4096], BF16)
    q0l_sp = dscr("q0l_sp", [4, 128, 4096], BF16)
    k0h_sp = dscr("k0h_sp", [4, 128, 4096], BF16)
    k0l_sp = dscr("k0l_sp", [4, 128, 4096], BF16)
    v0h_sp = dscr("v0h_sp", [16, 128, 1040], BF16)
    v0l_sp = dscr("v0l_sp", [16, 128, 1040], BF16)
    pvh_sp = dscr("pvh_sp", [16, 4, 128, 512], BF16)
    pvl_sp = dscr("pvl_sp", [16, 4, 128, 512], BF16)
    x1_sp = dscr("x1_sp", [4, 128, 4096])
    s_sp = dscr("s_sp", [4, 128, 4096])
    q1h_sp = dscr("q1h_sp", [8, 128, 2048], BF16)
    q1l_sp = dscr("q1l_sp", [8, 128, 2048], BF16)
    v1h_sp = dscr("v1h_sp", [16, 128, 1024], BF16)
    v1l_sp = dscr("v1l_sp", [16, 128, 1024], BF16)
    kq_sp = dscr("kq_sp", [64, 2048])
    qq_sp = dscr("qq_sp", [64, 2048])
    mp_ln = dscr("mp_ln", [2048])
    g_tok = dscr("g_tok", [128, 2048, 8])
    wf1_sp = dscr("wf1_sp", [128, 64])
    wf2_sp = dscr("wf2_sp", [9, 1024])
    buckh_sp = dscr("buckh_sp", [40, 128, 2048], BF16)
    buckl_sp = dscr("buckl_sp", [40, 128, 2048], BF16)
    grid_sp = dscr("grid_sp", [640, 128], U16)
    lg_sp = dscr("lg_sp", [128, 128])

    xn1h_tap = dscr("xn1h_tap", [128, 16384], BF16) if "xn1h_tap" in taps else None
    xn1l_tap = dscr("xn1l_tap", [128, 16384], BF16) if "xn1l_tap" in taps else None

    out_fm = nc.dram_tensor("out_fm", [8, 128, 2048], F32, kind="ExternalOutput").ap()

    with contextlib.ExitStack() as ctx:
        en = ctx.enter_context
        R1A = en(nc.sbuf_tensor("R1A", [128, 8192], F32))
        R1B = en(nc.sbuf_tensor("R1B", [128, 16384], BF16))
        R2 = en(nc.sbuf_tensor("R2", [128, 16640], BF16))
        R3 = en(nc.sbuf_tensor("R3", [128, 16640], BF16))
        R4 = en(nc.sbuf_tensor("R4", [128, 16384], BF16))
        R5 = en(nc.sbuf_tensor("R5", [128, 16384], BF16))
        SC_ID = en(nc.sbuf_tensor("SC_ID", [128, 128], F32))
        SC_B = en(nc.sbuf_tensor("SC_B", [128, 120], F32))
        SMR = en(nc.sbuf_tensor("SMR", [128, 2048], F32))
        SC_IDX = en(nc.sbuf_tensor("SC_IDX", [128, 640], U16))
        SC_TM = en(nc.sbuf_tensor("SC_TM", [128, 80], F32))
        SC_TM2 = en(nc.sbuf_tensor("SC_TM2", [128, 128], F32))
        SC_ONEC = en(nc.sbuf_tensor("SC_ONEC", [128, 1], BF16))
        SC_ONEF = en(nc.sbuf_tensor("SC_ONEF", [1, 128], F32))
        ONEF64T = en(nc.sbuf_tensor("ONEF64T", [65, 128], F32))
        ONEF64 = ONEF64T[64:65, :]
        SC_CST = en(nc.sbuf_tensor("SC_CST", [128, 4], F32))
        CONSTS = (1e-5, 0.125, 0.5, PI)
        for _k, _v in enumerate(CONSTS):
            nc.const_aps.register(F32, _v, SC_CST[:, _k:_k + 1])
        PB = [en(nc.psum_tensor("pb%d" % i, [128, 512], F32)) for i in range(8)]

        b = Builder(nc, ctx)

        def _gate(n):
            if upto < n:
                raise _Done()

        def mm(psum_ap, lhs, rhs, start, stop, waits=()):
            b.emit("tensor",
                   lambda e, p=psum_ap, l=lhs, r=rhs, s1=start, s2=stop:
                   e.matmul(p, l, r, start=s1, stop=s2), waits=waits, inc="T")
            return b.snap("T")

        def mm3(psum_ap, lh, ll, rh, rl, start, stop, waits=()):
            mm(psum_ap, lh, rh, start, False, waits=waits)
            mm(psum_ap, lh, rl, False, False)
            return mm(psum_ap, ll, rh, False, stop)

        def mm2r(psum_ap, lhs_exact, rh, rl, start, stop, waits=()):
            mm(psum_ap, lhs_exact, rh, start, False, waits=waits)
            return mm(psum_ap, lhs_exact, rl, False, stop)

        def mm2l(psum_ap, lh, ll, rhs_exact, start, stop, waits=()):
            mm(psum_ap, lh, rhs_exact, start, False, waits=waits)
            return mm(psum_ap, ll, rhs_exact, False, stop)

        def trans(psum_ap, in_ap, waits=()):
            k = in_ap.shape[0]
            b.emit("tensor", lambda e, p=psum_ap, i=in_ap, n=k:
                   e.transpose(p, i, SC_ID[0:n, 0:n]), waits=waits, inc="T")
            return b.snap("T")

        def act(out_ap, in_ap, func, bias=0.0, scale=1.0, waits=()):
            b.emit("scalar",
                   lambda e, o=out_ap, i=in_ap, f=func, ba=bias, sc=scale:
                   e.activation(o, i, f, bias=ba, scale=sc), waits=waits, inc="S")
            return b.snap("S")

        def vop(fn, waits=(), eng="vector"):
            b.emit(eng, fn, waits=waits, inc=ESEM[eng])
            return b.snap(ESEM[eng])

        def tt(eng, o, x, y, op, waits=()):
            return vop(lambda e, O=o, X=x, Y=y, P=op: e.tensor_tensor(O, X, Y, P),
                       waits, eng)

        def ts2(eng, o, x, s1, s2, op0, op1, waits=()):
            return vop(lambda e, O=o, X=x, A=s1, C=s2, P0=op0, P1=op1:
                       e.tensor_scalar(O, X, A, C, P0, P1), waits, eng)

        def ts1(eng, o, x, s1, op, waits=()):
            return vop(lambda e, O=o, X=x, A=s1, P=op:
                       e.tensor_single_scalar(O, X, A, P), waits, eng)

        def stt(eng, o, x, s, y, op0, op1, waits=()):
            return vop(lambda e, O=o, X=x, A=s, Y=y, P0=op0, P1=op1:
                       e.scalar_tensor_tensor(O, X, A, Y, P0, P1), waits, eng)

        def cp(eng, o, x, waits=()):
            return vop(lambda e, O=o, X=x: e.tensor_copy(O, X), waits, eng)

        def recip(o, x, waits=()):
            return vop(lambda e, O=o, X=x: e.reciprocal(O, X), waits, "vector")

        def red(o, x, op, absval=False, negate=False, waits=()):
            return vop(lambda e, O=o, X=x, P=op, av=absval, ng=negate:
                       e.tensor_reduce(O, X, AX, P, apply_absolute_value=av,
                                       negate=ng), waits, "vector")

        def mset(o, val, waits=(), eng="vector"):
            return vop(lambda e, O=o, V=val: e.memset(O, V), waits, eng)

        def split_ps(psum_ap, hi_ap, lo_ap, scale=None, bias_ap=None, waits=()):
            if scale is not None:
                act(hi_ap, psum_ap, AF.Copy, scale=float(scale), waits=waits)
                stt("vector", lo_ap, psum_ap, float(scale), hi_ap,
                    AL.mult, AL.subtract)
            elif bias_ap is not None:
                u = R1A[:, 2560:3072]
                ts1("vector", u, psum_ap, bias_ap, AL.add, waits=waits)
                act(hi_ap, u, AF.Copy)
                tt("gpsimd", lo_ap, u, hi_ap, AL.subtract)
            else:
                act(hi_ap, psum_ap, AF.Copy, waits=waits)
                stt("vector", lo_ap, psum_ap, 0.0, hi_ap, AL.add, AL.subtract)
            return b.snap("S", "V")

        def split_sb(src_ap, hi_ap, lo_ap, waits=(), lo_eng="vector"):
            act(hi_ap, src_ap, AF.Copy, waits=waits)
            tt(lo_eng, lo_ap, src_ap, hi_ap, AL.subtract)
            return b.snap("S", ESEM[lo_eng])

        BOFF = {}
        boff = 0
        for nm, w in [("bq", 2), ("bqc", 7), ("bkv", 2), ("bkc", 7), ("bo", 8),
                      ("fb", 8), ("bq1", 8), ("bk1", 8), ("bo1", 8), ("g1", 8),
                      ("be1", 8), ("g2", 8), ("be2", 8), ("eidx", 8),
                      ("c999", 8), ("b2e", 8)]:
            BOFF[nm] = boff
            boff += w
        assert boff <= 128

        def bias(nm, c=0):
            return SC_B[:, BOFF[nm] + c:BOFF[nm] + c + 1]

        ONES512F = SMR[32:33, 0:512]
        RR = {"bqr": SMR[32:33, 512:640], "bqr_r": SMR[32:33, 640:768],
              "bkr": SMR[32:33, 768:896], "bkr_r": SMR[32:33, 896:1024],
              "bc2q": SMR[32:33, 1024:1088], "bm": SMR[32:33, 1088:1096],
              "rb": SMR[32:33, 1096:1104]}
        ST_MU = SMR[64:65, 0:512]
        ST_RS = SMR[64:65, 512:1024]
        ST_TA = SMR[64:65, 1024:1536]
        ST_TB = SMR[64:65, 1536:2048]

        try:
            b.dma("sync", SC_ID[:], identf)
            b.dma("sync", SC_B[:], biases_d)
            b.dma("sync", SMR[0:1, 0:1024], rows_d[0:1, :])
            b.dma("sync", SMR[0:1, 1024:2048], rows_d[1:2, :])
            b.dma("sync", SMR[32:33, 512:1104], rows_d[2:3, 0:592])
            mset(SC_ONEC[:], 1.0)
            for _k, _v in enumerate(CONSTS):
                mset(SC_CST[:, _k:_k + 1], _v)
            mset(SC_ONEF[:], 1.0)
            mset(ONES512F, 1.0)
            mset(ONEF64, 1.0)

            _gate(1)
            xa = R1A[:, 6144:8192]
            for c in range(8):
                b.dma("sync", xa, x_fm[c])
                if c == 0:
                    cp("vector", R1A[:, 0:2048], xa)
                    act(R1A[:, 2048:4096], xa, AF.Square)
                else:
                    tt("vector", R1A[:, 0:2048], R1A[:, 0:2048], xa, AL.add)
                    act(R1A[:, 4096:6144], xa, AF.Square)
                    tt("vector", R1A[:, 2048:4096], R1A[:, 2048:4096],
                       R1A[:, 4096:6144], AL.add)
            split_sb(R1A[:, 0:2048], R4[:, 0:2048], R5[:, 0:2048])
            split_sb(R1A[:, 2048:4096], R4[:, 2048:4096], R5[:, 2048:4096])
            for sl in range(4):
                s0, s1 = sl * 512, (sl + 1) * 512
                mm2r(PB[6][0:1, 0:512], SC_ONEC[:], R4[:, s0:s1], R5[:, s0:s1],
                     True, True)
                mm2r(PB[7][0:1, 0:512], SC_ONEC[:], R4[:, 2048 + s0:2048 + s1],
                     R5[:, 2048 + s0:2048 + s1], True, True)
                mu = ST_MU
                rstd = ST_RS
                ts1("vector", mu, PB[6][0:1, 0:512], 1.0 / 1024.0, AL.mult)
                ts1("vector", ST_TB, PB[7][0:1, 0:512], 1.0 / 1024.0, AL.mult)
                tt("vector", ST_TA, mu, mu, AL.mult)
                tt("vector", ST_TA, ST_TB, ST_TA, AL.subtract)
                act(ST_TB, ST_TA, AF.Sqrt, bias=1e-5)
                recip(rstd, ST_TB)
                stt("vector", ST_TA, mu, -1.0, rstd, AL.mult, AL.mult)
                mm(PB[6][:, 0:512], ONEF64, rstd, True, True)
                mm(PB[7][:, 0:512], ONEF64, ST_TA, True, True)
                for c in range(8):
                    xp = R1A[:, 4096 + (c % 2) * 512:4096 + (c % 2) * 512 + 512]
                    b.dma("sync", xp, x_fm[c][:, s0:s1])
                    u = R1A[:, 5120:5632]
                    tt("vector", u, xp, PB[6][:, 0:512], AL.mult)
                    tt("vector", u, u, PB[7][:, 0:512], AL.add)
                    split_sb(u, R2[:, c * 2048 + s0:c * 2048 + s1],
                             R3[:, c * 2048 + s0:c * 2048 + s1], lo_eng="gpsimd")

            _gate(2)
            for pc in range(8):
                piece = R1A[:, 0:1024]
                b.dma("sync", piece, k0_d[:, pc, :])
                red(SC_TM[:, 8 + pc:9 + pc], piece, AL.max, absval=True)
            red(SC_TM[:, 0:1], SC_TM[:, 8:16], AL.max)
            mm(PB[0][0:1, 0:128], SC_TM[:, 0:1], SC_ID[:], True, True)
            red(SMR[64:65, 1024:1025], PB[0][0:1, 0:128], AL.max)
            ts1("vector", SMR[64:65, 1025:1026], SMR[64:65, 1024:1025], 1.0 / 7.0,
                AL.mult)
            recip(SMR[64:65, 1026:1027], SMR[64:65, 1025:1026])
            mm(PB[1][:, 0:2], ONEF64, SMR[64:65, 1025:1027], True, True)
            cp("vector", SC_TM[:, 1:3], PB[1][:, 0:2])
            scale_pp = SC_TM[:, 1:2]
            invs_pp = SC_TM[:, 2:3]
            for pc in range(8):
                piece = R1A[:, 0:1024]
                ws = R1A[:, 1024:2048]
                a1 = R1A[:, 2048:3072]
                a2 = R1A[:, 3072:4096]
                b.dma("sync", piece, k0_d[:, pc, :])
                ts1("vector", ws, piece, invs_pp, AL.mult)
                mset(a1, -7.0)
                cur, nxt = a1, a2
                for k in range(-7, 7):
                    eng = "vector" if k % 2 == 0 else "vector"
                    stt(eng, nxt, ws, float(k) + 0.5, cur, AL.is_ge, AL.add)
                    cur, nxt = nxt, cur
                ts2("vector", cur, cur, 7.0, -8.0, AL.min, AL.max)
                cp("vector", R4[:, pc * 1024:(pc + 1) * 1024], cur)

            b.dma("sync", R1A[:, 6144:6272], a_d)
            split_sb(R1A[:, 6144:6272], R5[:, 0:128], R5[:, 128:256])
            b.dma("sync", R1A[0:16, 4096:5120], b_mat)
            ts1("vector", R1A[0:16, 5120:6144], R1A[0:16, 4096:5120],
                SC_TM[0:16, 2:3], AL.mult)
            bp_h = R5[0:16, 256:1280]
            bp_l = R5[0:16, 1280:2304]
            split_sb(R1A[0:16, 5120:6144], bp_h, bp_l)
            t1h = R5[0:16, 2304:2816]
            t1l = R5[0:16, 2816:3328]

            for sl in range(4):
                s0, s1 = sl * 512, (sl + 1) * 512
                pt1 = PB[2][0:16, 0:512]
                for c in range(8):
                    mm3(pt1, R5[:, 0:128][:, c * 16:c * 16 + 16],
                        R5[:, 128:256][:, c * 16:c * 16 + 16],
                        R2[:, c * 2048 + s0:c * 2048 + s1],
                        R3[:, c * 2048 + s0:c * 2048 + s1],
                        start=(c == 0), stop=(c == 7))
                split_ps(pt1, t1h, t1l)
                for dc in range(8):
                    pb = PB[dc % 2][:, 0:512]
                    with b.relax():
                        tw = b.wall()
                        for c in range(8):
                            lhs = R4[:, c * 1024 + dc * 128:c * 1024 + dc * 128 + 128]
                            mm(pb, lhs, R2[:, c * 2048 + s0:c * 2048 + s1],
                               start=(c == 0), stop=False, waits=tw if c == 0 else ())
                            mm(pb, lhs, R3[:, c * 2048 + s0:c * 2048 + s1],
                               start=False, stop=False)
                        mm3(pb, bp_h[:, dc * 128:dc * 128 + 128],
                            bp_l[:, dc * 128:dc * 128 + 128], t1h, t1l,
                            start=False, stop=True)
                    gsl = R1A[:, 5120 + (dc % 2) * 512:5120 + (dc % 2) * 512 + 512]
                    act(gsl, pb, AF.Silu, bias=bias("fb", dc), scale=scale_pp)
                    b.dma("scalar", g0_sp[sl][:, dc * 512:dc * 512 + 512], gsl)

            _gate(3)
            def load_wsplit(dst_off, n, src_ap):
                st = R1A[:, 6144:6144 + n]
                b.dma("sync", st, src_ap)
                split_sb(st, R4[:, dst_off:dst_off + n], R5[:, dst_off:dst_off + n],
                         lo_eng="gpsimd")

            load_wsplit(8192, 1024, wqr_d.rearrange("p c f -> p (c f)"))
            load_wsplit(9216, 1024, wqr_r.rearrange("p c f -> p (c f)"))
            load_wsplit(10240, 1024, wkr_d.rearrange("p c f -> p (c f)"))
            load_wsplit(11264, 1024, wkr_r.rearrange("p c f -> p (c f)"))
            b.dma("sync", R1A[:, 0:4096], cossin_d)
            cosb = R1A[:, 0:2048]
            sinb = R1A[:, 2048:4096]
            for side in range(2):
                woff = 8192 if side == 0 else 10240
                woff_r = woff + 1024
                brow = "bqr" if side == 0 else "bkr"
                brow_r = brow + "_r"
                sph = q0h_sp if side == 0 else k0h_sp
                spl = q0l_sp if side == 0 else k0l_sp
                for sl in range(4):
                    s0, s1 = sl * 512, (sl + 1) * 512
                    p0 = PB[2][:, 0:512]
                    p1 = PB[3][:, 0:512]
                    for c in range(8):
                        mm3(p0, R4[:, woff + c * 128:woff + c * 128 + 128],
                            R5[:, woff + c * 128:woff + c * 128 + 128],
                            R2[:, c * 2048 + s0:c * 2048 + s1],
                            R3[:, c * 2048 + s0:c * 2048 + s1],
                            start=(c == 0), stop=False)
                    mm(p0, RR[brow], ONES512F, False, True)
                    for c in range(8):
                        mm3(p1, R4[:, woff_r + c * 128:woff_r + c * 128 + 128],
                            R5[:, woff_r + c * 128:woff_r + c * 128 + 128],
                            R2[:, c * 2048 + s0:c * 2048 + s1],
                            R3[:, c * 2048 + s0:c * 2048 + s1],
                            start=(c == 0), stop=False)
                    mm(p1, RR[brow_r], ONES512F, False, True)
                    u = R1A[:, 4096:4608]
                    u2 = R1A[:, 4608:5120]
                    tt("vector", u, p0, cosb[:, s0:s1], AL.mult)
                    tt("vector", u2, p1, sinb[:, s0:s1], AL.mult)
                    tt("vector", u, u, u2, AL.add)
                    g = R1A[:, 5120:5632]
                    act(g, u, AF.Gelu)
                    hi = R1B[:, 0:512]
                    lo = R1B[:, 512:1024]
                    if side == 0:
                        split_sb(g, hi, lo, lo_eng="gpsimd")
                    else:
                        act(hi, g, AF.Copy, scale=0.125)
                        stt("vector", lo, g, 0.125, hi, AL.mult, AL.subtract)
                    b.dma("scalar", sph[sl][:, 3584:4096], hi)
                    b.dma("scalar", spl[sl][:, 3584:4096], lo)

            _gate(4)
            load_wsplit(8192, 2048, wq_d.rearrange("p c f -> p (c f)"))
            load_wsplit(10240, 896, wqc_d[:, 0, :])
            load_wsplit(11136, 896, wqc_d[:, 1, :])
            load_wsplit(12032, 2048, wkv_d.rearrange("p c f -> p (c f)"))
            load_wsplit(14080, 896, wkc_d[:, 0, :])
            load_wsplit(14976, 896, wkc_d[:, 1, :])
            for sl in range(4):
                s0, s1 = sl * 512, (sl + 1) * 512
                for side in range(2):
                    woff = 8192 if side == 0 else 12032
                    wcoff = 10240 if side == 0 else 14080
                    bnm = "bq" if side == 0 else "bkv"
                    bcnm = "bqc" if side == 0 else "bkc"
                    sph = q0h_sp if side == 0 else k0h_sp
                    spl = q0l_sp if side == 0 else k0l_sp
                    for c2 in range(2):
                        p = PB[2][:, 0:512]
                        for c in range(8):
                            o = woff + c * 256 + c2 * 128
                            mm3(p, R4[:, o:o + 128], R5[:, o:o + 128],
                                R2[:, c * 2048 + s0:c * 2048 + s1],
                                R3[:, c * 2048 + s0:c * 2048 + s1],
                                start=(c == 0), stop=(c == 7))
                        u = R1A[:, 4096:4608]
                        act(u, p, AF.Gelu, bias=bias(bnm, c2))
                        if side == 0:
                            split_sb(u, R1B[:, c2 * 512:c2 * 512 + 512],
                                     R1B[:, 1024 + c2 * 512:1024 + c2 * 512 + 512],
                                     lo_eng="gpsimd")
                        else:
                            split_sb(u, R4[:, 4096 + c2 * 2048 + s0:4096 + c2 * 2048 + s1],
                                     R5[:, 4096 + c2 * 2048 + s0:4096 + c2 * 2048 + s1],
                                     lo_eng="gpsimd")
                    for f7 in range(7):
                        p = PB[3][:, 0:512]
                        for c2 in range(2):
                            if side == 0:
                                rh = R1B[:, c2 * 512:c2 * 512 + 512]
                                rl = R1B[:, 1024 + c2 * 512:1024 + c2 * 512 + 512]
                            else:
                                rh = R4[:, 4096 + c2 * 2048 + s0:4096 + c2 * 2048 + s1]
                                rl = R5[:, 4096 + c2 * 2048 + s0:4096 + c2 * 2048 + s1]
                            o = wcoff + c2 * 896 + f7 * 128
                            mm3(p, R4[:, o:o + 128], R5[:, o:o + 128], rh, rl,
                                start=(c2 == 0), stop=(c2 == 1))
                        u = R1A[:, 4608:5120]
                        act(u, p, AF.Gelu, bias=bias(bcnm, f7))
                        hi = R1B[:, 2048:2560]
                        lo = R1B[:, 2560:3072]
                        if side == 0:
                            split_sb(u, hi, lo, lo_eng="gpsimd")
                        else:
                            act(hi, u, AF.Copy, scale=0.125)
                            stt("vector", lo, u, 0.125, hi, AL.mult, AL.subtract)
                        b.dma("scalar", sph[sl][:, f7 * 512:f7 * 512 + 512], hi)
                        b.dma("scalar", spl[sl][:, f7 * 512:f7 * 512 + 512], lo)

            _gate(5)
            load_wsplit(0, 2048, wv_d.rearrange("p c f -> p (c f)"))
            vview_h = R1B[:, 0:1040].rearrange("p (h w) -> p h w", w=65)
            vview_l = R1B[:, 1040:2080].rearrange("p (h w) -> p h w", w=65)
            mset(vview_h[:, :, 64:65].rearrange("p h w -> p (h w)"), 1.0)
            mset(vview_l[:, :, 64:65].rearrange("p h w -> p (h w)"), 0.0)
            for tgt in range(16):
                for hh in range(2):
                    p = PB[2][:, 0:512]
                    for c2 in range(2):
                        mm3(p, R4[:, 4096 + c2 * 2048 + tgt * 128:4096 + c2 * 2048 + tgt * 128 + 128],
                            R5[:, 4096 + c2 * 2048 + tgt * 128:4096 + c2 * 2048 + tgt * 128 + 128],
                            R4[:, 0 + c2 * 1024 + hh * 512:0 + c2 * 1024 + hh * 512 + 512],
                            R5[:, 0 + c2 * 1024 + hh * 512:0 + c2 * 1024 + hh * 512 + 512],
                            start=(c2 == 0), stop=False)
                    mm(p, SC_ONEF[0:1, 0:128], SMR[0:1, hh * 512:hh * 512 + 512],
                       False, True)
                    oh = vview_h[:, hh * 8:hh * 8 + 8, 0:64]
                    ol = vview_l[:, hh * 8:hh * 8 + 8, 0:64]
                    split_ps(p, oh, ol)
                b.dma("scalar", v0h_sp[tgt], R1B[:, 0:1040])
                b.dma("scalar", v0l_sp[tgt], R1B[:, 1040:2080])
            for hp in range(8):
                st = R1A[:, 6144:7168]
                b.dma("sync", st, wo_d[:, hp, :])
                split_sb(st, R4[:, 8192 + hp * 1024:8192 + hp * 1024 + 1024],
                         R5[:, 8192 + hp * 1024:8192 + hp * 1024 + 1024],
                         lo_eng="gpsimd")

            _gate(6)
            for tgt in range(16):
                b.dma("sync", R2[:, tgt * 1040:(tgt + 1) * 1040], v0h_sp[tgt])
                b.dma("sync", R3[:, tgt * 1040:(tgt + 1) * 1040], v0l_sp[tgt])
            b.dma("sync", R4[:, 0:2048], masks_d.rearrange("p d q -> p (d q)"))
            MK = R4[:, 0:2048].rearrange("p (d q) -> p d q", q=512)
            for h in range(16):
                band = 64 * (h % 2)
                chn = h // 2
                for sl in range(4):
                    b.dma("sync", R1B[0:64, sl * 512:(sl + 1) * 512],
                          k0h_sp[sl][band:band + 64, chn * 512:(chn + 1) * 512])
                    b.dma("sync", R1B[0:64, 2048 + sl * 512:2048 + (sl + 1) * 512],
                          k0l_sp[sl][band:band + 64, chn * 512:(chn + 1) * 512])
                    b.dma("sync", R1B[0:64, 4096 + sl * 512:4096 + (sl + 1) * 512],
                          q0h_sp[sl][band:band + 64, chn * 512:(chn + 1) * 512])
                    b.dma("sync", R1B[0:64, 6144 + sl * 512:6144 + (sl + 1) * 512],
                          q0l_sp[sl][band:band + 64, chn * 512:(chn + 1) * 512])
                KHh = R1B[0:64, 0:2048]
                KHl = R1B[0:64, 2048:4096]
                QHh = R1B[0:64, 4096:6144]
                QHl = R1B[0:64, 6144:8192]
                for j in range(16):
                    i_lo = j // 4
                    sc_p = PB[0 if j % 2 == 0 else 1][:, 0:512]
                    for i in range(i_lo, 4):
                        with b.relax():
                            mm3(sc_p, KHh[:, j * 128:(j + 1) * 128],
                                KHl[:, j * 128:(j + 1) * 128],
                                QHh[:, i * 512:(i + 1) * 512],
                                QHl[:, i * 512:(i + 1) * 512],
                                start=True, stop=True)
                        Pt = R1A[:, 2048:2560]
                        if j // 4 == i:
                            ein = R1A[:, 0:512]
                            tt("vector", ein, sc_p, MK[:, j % 4, :], AL.add)
                            act(Pt, ein, AF.Exp)
                        else:
                            act(Pt, sc_p, AF.Exp)
                        Phi = R1B[:, 8192:8704]
                        Plo = R1B[:, 8704:9216]
                        act(Phi, Pt, AF.Copy)
                        tt("gpsimd", Plo, Pt, Phi, AL.subtract)
                        pv = PB[2 + i]
                        vh = R2[:, j * 1040 + h * 65:j * 1040 + h * 65 + 65]
                        vl = R3[:, j * 1040 + h * 65:j * 1040 + h * 65 + 65]
                        with b.relax():
                            tw = b.wall()
                            mm(pv[0:65, 0:512], vh, Phi, start=(j == 0),
                               stop=False, waits=tw)
                            mm(pv[0:65, 0:512], vh, Plo, start=False, stop=False)
                            mm(pv[0:65, 0:512], vl, Phi, start=False,
                               stop=(j == 4 * i + 3))
                for i in range(4):
                    pv = PB[2 + i]
                    nrow = ST_MU
                    cp("vector", nrow, pv[64:65, 0:512])
                    rn = ST_RS
                    recip(rn, nrow)
                    mm(PB[6][0:64, 0:512], ONEF64[0:1, 0:64], rn, True, True)
                    rnf = R1A[:, 3072:3584]
                    cp("vector", rnf[0:64, :], PB[6][0:64, 0:512])
                    pvn = R1A[:, 3584:4096]
                    tt("vector", pvn[0:64, :], pv[0:64, 0:512], rnf[0:64, :], AL.mult)
                    hi = R1B[:, 9216:9728]
                    lo = R1B[:, 9728:10240]
                    split_sb(pvn[0:64, :], hi[0:64, :], lo[0:64, :])
                    b.dma("scalar", pvh_sp[h][i][0:64, :], hi[0:64, :])
                    b.dma("scalar", pvl_sp[h][i][0:64, :], lo[0:64, :])
            for i in range(4):
                for hp in range(8):
                    b.dma("sync", R1B[0:64, hp * 512:(hp + 1) * 512],
                          pvh_sp[2 * hp][i][0:64, :])
                    b.dma("sync", R1B[64:128, hp * 512:(hp + 1) * 512],
                          pvh_sp[2 * hp + 1][i][0:64, :])
                    b.dma("sync", R1B[0:64, 8192 + hp * 512:8192 + (hp + 1) * 512],
                          pvl_sp[2 * hp][i][0:64, :])
                    b.dma("sync", R1B[64:128, 8192 + hp * 512:8192 + (hp + 1) * 512],
                          pvl_sp[2 * hp + 1][i][0:64, :])
                for dc in range(8):
                    p = PB[dc % 2][:, 0:512]
                    with b.relax():
                        tw = b.wall()
                        for hp in range(8):
                            mm3(p, R4[:, 8192 + hp * 1024 + dc * 128:8192 + hp * 1024 + dc * 128 + 128],
                                R5[:, 8192 + hp * 1024 + dc * 128:8192 + hp * 1024 + dc * 128 + 128],
                                R1B[:, hp * 512:(hp + 1) * 512],
                                R1B[:, 8192 + hp * 512:8192 + (hp + 1) * 512],
                                start=(hp == 0), stop=(hp == 7),
                                waits=tw if hp == 0 else ())
                    s0b = R1A[:, 2048:2560]
                    ts1("vector", s0b, p, bias("bo", dc), AL.add)
                    xpc = R1A[:, 2560:3072]
                    gpc = R1A[:, 3072:3584]
                    b.dma("sync", xpc, x_fm[dc][:, i * 512:(i + 1) * 512])
                    b.dma("sync", gpc, g0_sp[i][:, dc * 512:dc * 512 + 512])
                    t2 = R1A[:, 3584:4096]
                    tt("vector", t2, s0b, gpc, AL.mult)
                    x1p = R1A[:, 4096:4608]
                    tt("vector", x1p, xpc, t2, AL.add)
                    b.dma("scalar", x1_sp[i][:, dc * 512:dc * 512 + 512], x1p)

            _gate(7)
            for sl in range(4):
                b.dma("sync", R1A[:, 0:4096], x1_sp[sl])
                acc = R1A[:, 4096:4608]
                acq = R1A[:, 4608:5120]
                sq = R1A[:, 5120:5632]
                cp("vector", acc, R1A[:, 0:512])
                act(acq, R1A[:, 0:512], AF.Square)
                for c in range(1, 8):
                    tt("vector", acc, acc, R1A[:, c * 512:(c + 1) * 512], AL.add)
                    act(sq, R1A[:, c * 512:(c + 1) * 512], AF.Square)
                    tt("vector", acq, acq, sq, AL.add)
                split_sb(acc, R1B[:, 0:512], R1B[:, 512:1024])
                split_sb(acq, R1B[:, 1024:1536], R1B[:, 1536:2048])
                mm2r(PB[6][0:1, 0:512], SC_ONEC[:], R1B[:, 0:512], R1B[:, 512:1024],
                     True, True)
                mm2r(PB[7][0:1, 0:512], SC_ONEC[:], R1B[:, 1024:1536],
                     R1B[:, 1536:2048], True, True)
                s0, s1 = sl * 512, (sl + 1) * 512
                ts1("vector", ST_MU, PB[6][0:1, 0:512], 1.0 / 1024.0, AL.mult)
                ts1("vector", ST_TB, PB[7][0:1, 0:512], 1.0 / 1024.0, AL.mult)
                tt("vector", ST_TA, ST_MU, ST_MU, AL.mult)
                tt("vector", ST_TA, ST_TB, ST_TA, AL.subtract)
                act(ST_TB, ST_TA, AF.Sqrt, bias=1e-5)
                recip(ST_RS, ST_TB)
                stt("vector", ST_TA, ST_MU, -1.0, ST_RS, AL.mult, AL.mult)
                mm(PB[6][:, 0:512], ONEF64, ST_RS, True, True)
                mm(PB[7][:, 0:512], ONEF64, ST_TA, True, True)
                for c in range(8):
                    u = R1A[:, 5632:6144]
                    tt("vector", u, R1A[:, c * 512:(c + 1) * 512], PB[6][:, 0:512],
                       AL.mult)
                    tt("vector", u, u, PB[7][:, 0:512], AL.add)
                    split_sb(u, R2[:, c * 2048 + s0:c * 2048 + s1],
                             R3[:, c * 2048 + s0:c * 2048 + s1], lo_eng="gpsimd")
            if xn1h_tap is not None:
                b.dma("sync", xn1h_tap, R2[:, 0:16384])
                b.dma("sync", xn1l_tap, R3[:, 0:16384])

            _gate(8)
            b.dma("sync", R1A[:, 3072:3104], b1_d)
            b.dma("sync", R1A[:, 3104:3136], bqo_d)
            for cf in range(32):
                st = R1A[:, 0:1024]
                b.dma("sync", st, w1t_d[:, cf, :])
                split_sb(st, R1B[:, 0:1024], R1B[:, 1024:2048], lo_eng="gpsimd")
                st2 = R1A[:, 1024:1032]
                b.dma("sync", st2, wm_d[:, cf, :])
                split_sb(st2, R1B[:, 2048:2056], R1B[:, 2056:2064])
                split_sb(R1A[:, 3072 + cf:3073 + cf], R1B[:, 2064:2065],
                         R1B[:, 2065:2066])
                for half in range(2):
                    mm3(PB[half][0:8, 0:512],
                        R1B[:, 2048:2056], R1B[:, 2056:2064],
                        R1B[:, half * 512:half * 512 + 512],
                        R1B[:, 1024 + half * 512:1024 + half * 512 + 512],
                        start=(cf == 0), stop=(cf == 31))
                mm3(PB[2][0:8, 0:1], R1B[:, 2048:2056], R1B[:, 2056:2064],
                    R1B[:, 2064:2065], R1B[:, 2065:2066],
                    start=(cf == 0), stop=False)
                if cf == 31:
                    mm(PB[2][0:8, 0:1], RR["bm"], ONES512F[0:1, 0:1], False, True)
                st3 = R1A[:, 1032:1041]
                b.dma("sync", st3[:, 0:8], wqot_d[:, cf, :])
                cp("vector", st3[:, 8:9], R1A[:, 3104 + cf:3105 + cf])
                split_sb(st3, R1B[:, 2066:2075], R1B[:, 2075:2084])
                st4 = R1A[:, 2048:3072]
                b.dma("sync", st4, w2_d[:, cf, :])
                split_sb(st4, R1B[:, 2084:3108], R1B[:, 3108:4132], lo_eng="gpsimd")
                for half in range(2):
                    mm3(PB[4 + half][0:9, 0:512],
                        R1B[:, 2066:2075], R1B[:, 2075:2084],
                        R1B[:, 2084 + half * 512:2084 + half * 512 + 512],
                        R1B[:, 3108 + half * 512:3108 + half * 512 + 512],
                        start=(cf == 0), stop=(cf == 31))
            cp("vector", SC_TM[0:8, 3:4], PB[2][0:8, 0:1])
            cp("vector", R1A[0:8, 0:512], PB[0][0:8, 0:512])
            cp("vector", R1A[0:8, 512:1024], PB[1][0:8, 0:512])
            for dc in range(8):
                trans(PB[6][:, 0:8], R1A[0:8, dc * 128:dc * 128 + 128])
                cp("vector", R1A[:, 1024 + dc * 8:1024 + dc * 8 + 8], PB[6][:, 0:8])
            b.dma("scalar", wf1_sp, R1A[:, 1024:1088])
            cp("vector", R1A[0:9, 64:576], PB[4][0:9, 0:512])
            cp("vector", R1A[0:9, 576:1088], PB[5][0:9, 0:512])
            b.dma("scalar", wf2_sp[:, 0:512], R1A[0:9, 64:576])
            b.dma("scalar", wf2_sp[:, 512:1024], R1A[0:9, 576:1088])

            _gate(9)
            b.dma("sync", R1A[:, 0:64], wc2q_d.rearrange("p h q -> p (h q)"))
            split_sb(R1A[:, 0:64], R1B[:, 4096:4160], R1B[:, 4160:4224])
            for side in range(2):
                wsrc = wq1_d if side == 0 else wk1_d
                bcol = "bq1" if side == 0 else "bk1"
                for h in range(8):
                    st = R1A[:, 1024:2048]
                    b.dma("sync", st, wsrc[:, :, h, :])
                    split_sb(st, R1B[:, 0:1024], R1B[:, 1024:2048], lo_eng="gpsimd")
                    for sl in range(4):
                        s0, s1 = sl * 512, (sl + 1) * 512
                        p = PB[2][:, 0:512]
                        for c in range(8):
                            mm3(p, R1B[:, c * 128:c * 128 + 128],
                                R1B[:, 1024 + c * 128:1024 + c * 128 + 128],
                                R2[:, c * 2048 + s0:c * 2048 + s1],
                                R3[:, c * 2048 + s0:c * 2048 + s1],
                                start=(c == 0), stop=(c == 7))
                        if side == 0:
                            split_ps(p, R1B[:, 2048 + s0:2048 + s1],
                                     R1B[:, 4224 + s0:4224 + s1],
                                     bias_ap=bias(bcol, h))
                        else:
                            split_ps(p, R4[:, 2048 * h + s0:2048 * h + s1],
                                     R5[:, 2048 * h + s0:2048 * h + s1],
                                     bias_ap=bias(bcol, h))
                    if side == 0:
                        Qh = R1B[:, 2048:4096]
                        Ql = R1B[:, 4224:6272]
                        b.dma("scalar", q1h_sp[h], Qh)
                        b.dma("scalar", q1l_sp[h], Ql)
                    for tc in range(16):
                        if side == 0:
                            lh = Qh[:, tc * 128:(tc + 1) * 128]
                            ll = Ql[:, tc * 128:(tc + 1) * 128]
                        else:
                            lh = R4[:, 2048 * h + tc * 128:2048 * h + (tc) * 128 + 128]
                            ll = R5[:, 2048 * h + tc * 128:2048 * h + (tc) * 128 + 128]
                        pq = PB[3][:, tc * 8:tc * 8 + 8]
                        mm3(pq, lh, ll,
                            R1B[:, 4096 + h * 8:4096 + h * 8 + 8],
                            R1B[:, 4160 + h * 8:4160 + h * 8 + 8],
                            start=True, stop=False)
                        mm(pq, ONES512F[0:1, 0:128],
                           RR["bc2q"][0:1, h * 8:h * 8 + 8], False, True)
                    act(R1A[:, 0:128], PB[3][:, 0:128], AF.Square)
                    red(R1A[:, 128:144],
                        R1A[:, 0:128].rearrange("p (a c) -> p a c", c=8), AL.add)
                    act(R1A[:, 144:160], R1A[:, 128:144], AF.Sqrt)
                    recip(R1A[:, 160:176], R1A[:, 144:160])
                    for tc in range(16):
                        ts1("vector", R1A[:, 304 + tc * 8:304 + tc * 8 + 8],
                            PB[3][:, tc * 8:tc * 8 + 8],
                            R1A[:, 160 + tc:161 + tc], AL.mult)
                        trans(PB[0][0:8, 0:128], R1A[:, 304 + tc * 8:304 + tc * 8 + 8])
                        cp("vector", SC_TM2[0:8, 0:128], PB[0][0:8, 0:128])
                        b.dma("scalar",
                              (qq_sp if side == 0 else kq_sp)[h * 8:h * 8 + 8,
                                                              tc * 128:(tc + 1) * 128],
                              SC_TM2[0:8, 0:128])
            for half in range(2):
                for c in range(8):
                    st = R1A[:, 1024:1536]
                    b.dma("sync", st, wv1_d[:, c, half * 512:half * 512 + 512])
                    split_sb(st, R1B[:, 4224 + c * 512:4224 + c * 512 + 512],
                             R1B[:, 8320 + c * 512:8320 + c * 512 + 512],
                             lo_eng="gpsimd")
                for tgt in range(16):
                    p = PB[2][:, 0:512]
                    for c in range(8):
                        mm3(p, R2[:, c * 2048 + tgt * 128:c * 2048 + tgt * 128 + 128],
                            R3[:, c * 2048 + tgt * 128:c * 2048 + tgt * 128 + 128],
                            R1B[:, 4224 + c * 512:4224 + c * 512 + 512],
                            R1B[:, 8320 + c * 512:8320 + c * 512 + 512],
                            start=(c == 0), stop=False)
                    mm(p, SC_ONEF[0:1, 0:128],
                       SMR[0:1, 1024 + half * 512:1024 + half * 512 + 512],
                       False, True)
                    split_ps(p, R1B[:, 0:512], R1B[:, 512:1024])
                    b.dma("scalar", v1h_sp[tgt][:, half * 512:half * 512 + 512],
                          R1B[:, 0:512])
                    b.dma("scalar", v1l_sp[tgt][:, half * 512:half * 512 + 512],
                          R1B[:, 512:1024])

            _gate(10)
            SCALE1 = 0.5 / math.sqrt(128.0)
            for h in range(8):
                b.dma("sync", R1B[:, 0:2048], q1h_sp[h])
                b.dma("sync", R1B[:, 2048:4096], q1l_sp[h])
                b.dma("sync", R1A[0:8, 4096:6144], kq_sp[h * 8:h * 8 + 8, :])
                b.dma("sync", R1A[0:8, 6144:8192], qq_sp[h * 8:h * 8 + 8, :])
                for ip in range(2):
                    for j in range(16):
                        b.dma("sync", R1B[:, 4096 + (j % 2) * 1024:4096 + (j % 2) * 1024 + 1024],
                              v1h_sp[j])
                        b.dma("sync", R1B[:, 6144 + (j % 2) * 1024:6144 + (j % 2) * 1024 + 1024],
                              v1l_sp[j])
                        for ii in range(2):
                            i = ip * 2 + ii
                            mm(PB[0][:, 0:512],
                               R1A[0:8, 4096 + j * 128:4096 + j * 128 + 128],
                               R1A[0:8, 6144 + i * 512:6144 + i * 512 + 512],
                               True, True)
                            mm3(PB[1][:, 0:512],
                                R4[:, 2048 * h + j * 128:2048 * h + j * 128 + 128],
                                R5[:, 2048 * h + j * 128:2048 * h + j * 128 + 128],
                                R1B[:, i * 512:(i + 1) * 512],
                                R1B[:, 2048 + i * 512:2048 + i * 512 + 512],
                                start=True, stop=True)
                            fidc = R1A[:, 0:512]
                            ts2("vector", fidc, PB[0][:, 0:512], 1.0, -1.0,
                                AL.min, AL.max)
                            stile = R1A[:, 512:1024]
                            act(stile, fidc, AF.Sin, scale=PI)
                            u = R1A[:, 1024:1536]
                            tt("gpsimd", u, fidc, stile, AL.mult)
                            t1 = R1A[:, 1536:2048]
                            stt("vector", t1, u, 0.1, fidc, AL.mult, AL.add)
                            ein = R1A[:, 2048:2560]
                            stt("vector", ein, PB[1][:, 0:512], 2.0 * SCALE1, t1,
                                AL.mult, AL.add)
                            Pt = R1A[:, 2560:3072]
                            act(Pt, ein, AF.Exp, scale=0.5)
                            Phi = R1B[:, 8192:8704]
                            Plo = R1B[:, 8704:9216]
                            act(Phi, Pt, AF.Copy)
                            tt("gpsimd", Plo, Pt, Phi, AL.subtract)
                            mm2r(PB[6 + ii][0:1, 0:512], SC_ONEC[:], Phi, Plo,
                                 start=(j == 0), stop=(j == 15))
                            vt = R1B[:, 4096 + (j % 2) * 1024 + h * 128:
                                     4096 + (j % 2) * 1024 + h * 128 + 128]
                            vtl = R1B[:, 6144 + (j % 2) * 1024 + h * 128:
                                      6144 + (j % 2) * 1024 + h * 128 + 128]
                            pv = PB[2 + ii][:, 0:512]
                            with b.relax():
                                tw = b.wall()
                                mm(pv, vt, Phi, start=(j == 0), stop=False, waits=tw)
                                mm(pv, vt, Plo, start=False, stop=False)
                                mm(pv, vtl, Phi, start=False, stop=(j == 15))
                    for ii in range(2):
                        i = ip * 2 + ii
                        nrow = ST_MU
                        cp("vector", nrow, PB[6 + ii][0:1, 0:512])
                        rn = ST_RS
                        recip(rn, nrow)
                        mm(PB[4][:, 0:512], ONEF64, rn, True, True)
                        rnf = R1A[:, 3072:3584]
                        cp("vector", rnf, PB[4][:, 0:512])
                        pvn = R1A[:, 3584:4096]
                        tt("vector", pvn, PB[2 + ii][:, 0:512], rnf, AL.mult)
                        split_sb(pvn, R1B[:, 8192:8704], R1B[:, 8704:9216])
                        b.dma("scalar", pvh_sp[h][i], R1B[:, 8192:8704])
                        b.dma("scalar", pvl_sp[h][i], R1B[:, 8704:9216])
            _gate(91)
            for hh in range(8):
                st = R1A[:, 0:1024]
                b.dma("sync", st, wo1_d[:, hh, :])
                split_sb(st, R4[:, hh * 1024:hh * 1024 + 1024],
                         R5[:, hh * 1024:hh * 1024 + 1024], lo_eng="gpsimd")
            _gate(92)
            for i in range(4):
                for hh in range(8):
                    b.dma("sync", R1B[:, hh * 512:(hh + 1) * 512], pvh_sp[hh][i])
                    b.dma("sync", R1B[:, 8192 + hh * 512:8192 + (hh + 1) * 512],
                          pvl_sp[hh][i])
                s0, s1 = i * 512, (i + 1) * 512
                for dc in range(8):
                    p = PB[dc % 2][:, 0:512]
                    with b.relax():
                        tw = b.wall()
                        for hh in range(8):
                            mm3(p, R4[:, hh * 1024 + dc * 128:hh * 1024 + dc * 128 + 128],
                                R5[:, hh * 1024 + dc * 128:hh * 1024 + dc * 128 + 128],
                                R1B[:, hh * 512:(hh + 1) * 512],
                                R1B[:, 8192 + hh * 512:8192 + (hh + 1) * 512],
                                start=(hh == 0), stop=(hh == 7),
                                waits=tw if hh == 0 else ())
                    pre = R1A[:, dc * 512:dc * 512 + 512]
                    ts1("vector", pre, p, bias("bo1", dc), AL.add)
                    tt("vector", pre, pre, R2[:, dc * 2048 + s0:dc * 2048 + s1], AL.add)
                    tt("vector", pre, pre, R3[:, dc * 2048 + s0:dc * 2048 + s1], AL.add)
                _gate(93)
                acc = R1A[:, 4096:4608]
                acq = R1A[:, 4608:5120]
                sq = R1A[:, 5120:5632]
                cp("vector", acc, R1A[:, 0:512])
                act(acq, R1A[:, 0:512], AF.Square)
                for c in range(1, 8):
                    tt("vector", acc, acc, R1A[:, c * 512:(c + 1) * 512], AL.add)
                    act(sq, R1A[:, c * 512:(c + 1) * 512], AF.Square)
                    tt("vector", acq, acq, sq, AL.add)
                split_sb(acc, R1B[:, 9216:9728], R1B[:, 9728:10240])
                split_sb(acq, R1B[:, 10240:10752], R1B[:, 10752:11264])
                mm2r(PB[6][0:1, 0:512], SC_ONEC[:], R1B[:, 9216:9728],
                     R1B[:, 9728:10240], True, True)
                mm2r(PB[7][0:1, 0:512], SC_ONEC[:], R1B[:, 10240:10752],
                     R1B[:, 10752:11264], True, True)
                ts1("vector", ST_MU, PB[6][0:1, 0:512], 1.0 / 1024.0, AL.mult)
                ts1("vector", ST_TB, PB[7][0:1, 0:512], 1.0 / 1024.0, AL.mult)
                tt("vector", ST_TA, ST_MU, ST_MU, AL.mult)
                tt("vector", ST_TA, ST_TB, ST_TA, AL.subtract)
                act(ST_TB, ST_TA, AF.Sqrt, bias=1e-5)
                recip(ST_RS, ST_TB)
                stt("vector", ST_TA, ST_MU, -1.0, ST_RS, AL.mult, AL.mult)
                mm(PB[6][:, 0:512], ONEF64, ST_RS, True, True)
                mm(PB[7][:, 0:512], ONEF64, ST_TA, True, True)
                _gate(94)
                for dc in range(8):
                    u = R1A[:, 5632:6144]
                    tt("vector", u, R1A[:, dc * 512:dc * 512 + 512], PB[6][:, 0:512],
                       AL.mult)
                    tt("vector", u, u, PB[7][:, 0:512], AL.add)
                    ts2("vector", R1A[:, dc * 512:dc * 512 + 512], u,
                        bias("g1", dc), bias("be1", dc), AL.mult, AL.add)
                _gate(95)
                for dc in range(8):
                    if upto == 96:
                        mset(R1B[:, dc * 512:dc * 512 + 512], 0.0)
                        mset(R1B[:, 4096 + dc * 512:4096 + dc * 512 + 512], 0.0,
                             eng="gpsimd")
                    else:
                        split_sb(R1A[:, dc * 512:dc * 512 + 512],
                                 R1B[:, dc * 512:dc * 512 + 512],
                                 R1B[:, 4096 + dc * 512:4096 + dc * 512 + 512],
                                 lo_eng="gpsimd")
                _gate(96.5)
                _gate(98)
                b.dma("sync", SC_TM2[:, 0:64], wf1_sp)
                split_sb(SC_TM2[:, 0:64], R1B[:, 8192:8256], R1B[:, 8256:8320])
                pq = PB[2][0:8, 0:512]
                for dc in range(8):
                    mm3(pq, R1B[:, 8192 + dc * 8:8192 + dc * 8 + 8],
                        R1B[:, 8256 + dc * 8:8256 + dc * 8 + 8],
                        R1B[:, dc * 512:dc * 512 + 512],
                        R1B[:, 4096 + dc * 512:4096 + dc * 512 + 512],
                        start=(dc == 0), stop=(dc == 7))
                tanh = R1A[0:8, 4096:4608]
                act(tanh, pq, AF.Tanh, bias=SC_TM[0:8, 3:4])
                sarg = R1A[0:8, 4608:5120]
                ts2("vector", sarg, pq, 1.0, SC_TM[0:8, 3:4], AL.mult, AL.add)
                ts2("vector", sarg, sarg, PI / 8.0, PI, AL.mult, AL.min)
                ts1("vector", sarg, sarg, -PI, AL.max)
                ssin = R1A[0:8, 5120:5632]
                act(ssin, sarg, AF.Sin)
                tt("vector", sarg, ssin, ssin, AL.mult)
                ts2("vector", sarg, sarg, -2.0, 1.0, AL.mult, AL.add)
                qe = R1A[0:8, 5632:6144]
                tt("vector", qe, tanh, sarg, AL.mult)
                b.dma("sync", R1A[8:9, 5632:6144], ONES512F)
                split_sb(R1A[0:9, 5632:6144], R1B[0:9, 8320:8832],
                         R1B[0:9, 8832:9344])
                b.dma("sync", R1A[0:9, 4096:5120], wf2_sp)
                split_sb(R1A[0:9, 4096:5120], R1B[0:9, 9344:10368],
                         R1B[0:9, 10368:11392])
                for dc in range(8):
                    p = PB[3][:, 0:512]
                    mm3(p, R1B[0:9, 9344 + dc * 128:9344 + dc * 128 + 128],
                        R1B[0:9, 10368 + dc * 128:10368 + dc * 128 + 128],
                        R1B[0:9, 8320:8832], R1B[0:9, 8832:9344],
                        start=True, stop=True)
                    pre2 = R1A[:, 4096 + dc * 512:4096 + dc * 512 + 512]
                    ts1("vector", pre2, p, bias("b2e", dc), AL.add)
                    tt("vector", pre2, pre2, R1A[:, dc * 512:dc * 512 + 512], AL.add)
                acc = R1A[:, 0:512]
                acq = R1A[:, 512:1024]
                sq = R1A[:, 1024:1536]
                cp("vector", acc, R1A[:, 4096:4608])
                act(acq, R1A[:, 4096:4608], AF.Square)
                for c in range(1, 8):
                    tt("vector", acc, acc, R1A[:, 4096 + c * 512:4096 + (c + 1) * 512],
                       AL.add)
                    act(sq, R1A[:, 4096 + c * 512:4096 + (c + 1) * 512], AF.Square)
                    tt("vector", acq, acq, sq, AL.add)
                split_sb(acc, R1B[:, 9216:9728], R1B[:, 9728:10240])
                split_sb(acq, R1B[:, 10240:10752], R1B[:, 10752:11264])
                mm2r(PB[6][0:1, 0:512], SC_ONEC[:], R1B[:, 9216:9728],
                     R1B[:, 9728:10240], True, True)
                mm2r(PB[7][0:1, 0:512], SC_ONEC[:], R1B[:, 10240:10752],
                     R1B[:, 10752:11264], True, True)
                ts1("vector", ST_MU, PB[6][0:1, 0:512], 1.0 / 1024.0, AL.mult)
                ts1("vector", ST_TB, PB[7][0:1, 0:512], 1.0 / 1024.0, AL.mult)
                tt("vector", ST_TA, ST_MU, ST_MU, AL.mult)
                tt("vector", ST_TA, ST_TB, ST_TA, AL.subtract)
                act(ST_TB, ST_TA, AF.Sqrt, bias=1e-5)
                recip(ST_RS, ST_TB)
                stt("vector", ST_TA, ST_MU, -1.0, ST_RS, AL.mult, AL.mult)
                mm(PB[6][:, 0:512], ONEF64, ST_RS, True, True)
                mm(PB[7][:, 0:512], ONEF64, ST_TA, True, True)
                for dc in range(8):
                    u = R1A[:, 1536:2048]
                    tt("vector", u, R1A[:, 4096 + dc * 512:4096 + dc * 512 + 512],
                       PB[6][:, 0:512], AL.mult)
                    tt("vector", u, u, PB[7][:, 0:512], AL.add)
                    ts2("vector", R1A[:, 2048:2560], u, bias("g2", dc),
                        bias("be2", dc), AL.mult, AL.add)
                    b.dma("scalar", s_sp[i][:, dc * 512:dc * 512 + 512],
                          R1A[:, 2048:2560])

            _gate(100)
            CAP = 1280
            SLOTS = 8 * CAP
            mset(R1A[:, 0:4096], 0.0)
            for qq4 in range(4):
                b.dma("sync", g_tok[:, qq4 * 512:(qq4 + 1) * 512, :]
                      .rearrange("p t g -> p (t g)"), R1A[:, 0:4096])
            mset(R1B[:, 0:128], 1.0)
            b.dma("sync", R1A[:, 0:64], rw_d.rearrange("p c e -> p (c e)"))
            split_sb(R1A[:, 0:64], R1B[:, 128:192], R1B[:, 192:256])
            b.dma("sync", R1B[:, 384:512], tri128_d)
            b.dma("sync", R1A[:, 0:512], bcfree_d)
            for tc in range(16):
                pl = PB[0][:, tc * 8:tc * 8 + 8]
                for c in range(8):
                    mm3(pl, R2[:, c * 2048 + tc * 128:c * 2048 + tc * 128 + 128],
                        R3[:, c * 2048 + tc * 128:c * 2048 + tc * 128 + 128],
                        R1B[:, 128 + c * 8:128 + c * 8 + 8],
                        R1B[:, 192 + c * 8:192 + c * 8 + 8],
                        start=(c == 0), stop=False)
                mm(pl, ONES512F[0:1, 0:128], RR["rb"], False, True)
                cp("vector", R1A[:, 600:608], pl)
                b.dma("scalar", dram["lg_sp"][:, tc * 8:tc * 8 + 8], R1A[:, 600:608])
                m = R1A[:, 512:513]
                red(m, pl, AL.max)
                nm = R1A[:, 513:514]
                red(nm, pl, AL.max, negate=True)
                ex = R1A[:, 514:522]
                act(ex, pl, AF.Exp, bias=nm)
                sm = R1A[:, 522:523]
                red(sm, ex, AL.add)
                recip(SC_TM[:, 16 + tc:17 + tc], sm)
                eq = R1A[:, 523:531]
                ts1("vector", eq, pl, m, AL.is_ge)
                t9 = R1A[:, 531:539]
                tt("vector", t9, eq, SC_B[:, BOFF["c999"]:BOFF["c999"] + 8], AL.mult)
                v9 = R1A[:, 539:540]
                red(v9, t9, AL.max)
                ts2("vector", SC_TM[:, 32 + tc:33 + tc], v9, -1.0, 999.0,
                    AL.mult, AL.add)
                ts1("vector", R1A[:, 540:548],
                    SC_B[:, BOFF["eidx"]:BOFF["eidx"] + 8],
                    SC_TM[:, 32 + tc:33 + tc], AL.is_equal)
                cp("vector", R1B[:, 256 + tc * 8:256 + tc * 8 + 8], R1A[:, 540:548])
            for tc in range(16):
                pr = PB[1][:, tc * 8:tc * 8 + 8]
                if tc == 0:
                    mm(pr, R1B[:, 384:512], R1B[:, 256:264], True, True)
                else:
                    for tcp in range(tc):
                        mm(pr, R1B[:, 0:128], R1B[:, 256 + tcp * 8:256 + tcp * 8 + 8],
                           start=(tcp == 0), stop=False)
                    mm(pr, R1B[:, 384:512], R1B[:, 256 + tc * 8:256 + tc * 8 + 8],
                       False, True)
                t9 = R1A[:, 548:556]
                tt("vector", t9, pr, R1B[:, 256 + tc * 8:256 + tc * 8 + 8], AL.mult)
                red(SC_TM[:, 48 + tc:49 + tc], t9, AL.add)
                vm = R1A[:, 556:557]
                ts1("vector", vm, SC_TM[:, 48 + tc:49 + tc], float(CAP) - 0.5, AL.is_lt)
                bcv = R1A[:, 557:558]
                stt("vector", bcv, SC_TM[:, 32 + tc:33 + tc], float(CAP),
                    SC_TM[:, 48 + tc:49 + tc], AL.mult, AL.add)
                tt("vector", bcv, bcv, vm, AL.mult)
                u9 = R1A[:, 558:559]
                ts2("vector", u9, vm, -float(SLOTS), float(SLOTS), AL.mult, AL.add)
                tt("vector", SC_TM[:, 64 + tc:65 + tc], bcv, u9, AL.add)
            trans(PB[2][0:16, 0:128], SC_TM[:, 16:32])
            cp("vector", SC_TM2[0:16, :], PB[2][0:16, 0:128])
            b.dma("gpsimd", bass.AP(dram["mp_ln"].tensor, 0, [[128, 16], [1, 128]]),
                  SC_TM2[0:16, :])
            _gate(101)
            b.dma("sync", R1B[:, 512:560], invvals_d.rearrange("p t c -> p (t c)"))
            NB = SLOTS // 128
            NG = SLOTS // 512
            for g in range(NG):
                for tc in range(16):
                    Ptile = R1B[:, 1024:1536]
                    ts2("vector", Ptile, R1A[:, 0:512], float(512 * g),
                        SC_TM[:, 64 + tc:65 + tc], AL.add, AL.is_equal)
                    for bq in range(4):
                        mm(PB[3 + bq][:, g * 3:g * 3 + 3],
                           Ptile[:, bq * 128:bq * 128 + 128],
                           R1B[:, 512 + tc * 3:512 + tc * 3 + 3],
                           start=(tc == 0), stop=(tc == 15))
            for bq in range(4):
                cp("vector",
                   R1A[:, 512:512 + 3 * NB].rearrange("p (g b c) -> p g b c",
                                                      b=4, c=3)[:, :, bq, :],
                   PB[3 + bq][:, 0:3 * NG])
            hiv = R1A[:, 512:512 + 3 * NB].rearrange("p (b c) -> p b c", c=3)
            stt("vector", R1A[:, 752:752 + NB], hiv[:, :, 0], 128.0, hiv[:, :, 1],
                AL.mult, AL.add)
            ts2("vector", R1A[:, 832:832 + NB], R1A[:, 752:752 + NB], -1.0, 0.0,
                AL.add, AL.max)
            ts1("vector", R1A[:, 912:912 + NB], R1A[:, 832:832 + NB], 8.0, AL.mult)
            trans(PB[2][0:NB, 0:128], R1A[:, 912:912 + NB])
            cp("vector", SC_TM2[0:NB, :], PB[2][0:NB, 0:128])
            b.dma("gpsimd",
                  bass.AP(dram["grid_sp"].tensor, 0, [[1024, NB], [128, 8], [1, 16]]),
                  SC_TM2[0:NB, :].rearrange("b (h l) -> b h l", l=16))
            b.dma("sync", SC_IDX[:, 0:SLOTS // 16], dram["grid_sp"], transpose=True)
            b.dma("sync", SC_IDX[16:32, :], SC_IDX[0:16, :])
            b.dma("sync", SC_IDX[32:64, :], SC_IDX[0:32, :])
            b.dma("sync", SC_IDX[64:128, :], SC_IDX[0:64, :])
            _gate(102)
            for c in range(8):
                cp("vector", R4[:, 0:16384].rearrange("p (t g) -> p t g", g=8)[:, :, c],
                   R2[:, c * 2048:(c + 1) * 2048])
                cp("gpsimd", R5[:, 0:16384].rearrange("p (t g) -> p t g", g=8)[:, :, c],
                   R3[:, c * 2048:(c + 1) * 2048])
            for bi in range(40):
                e, kb = bi // 5, bi % 5
                icol = (CAP // 16) * e + 16 * kb
                for hf in range(2):
                    idx = SC_IDX[:, icol + hf * 8:icol + hf * 8 + 8]
                    b.emit("gpsimd",
                           lambda ee, O=R1B[:, hf * 1024:hf * 1024 + 1024]
                           .rearrange("p (i g) -> p i g", g=8),
                           D=R4[:, 0:16384].rearrange("p (t g) -> p t g", g=8),
                           I=idx: ee.indirect_copy(O, D, I, True),
                           waits=b.wall(), inc="G")
                    b.emit("gpsimd",
                           lambda ee, O=R1B[:, 2048 + hf * 1024:2048 + hf * 1024 + 1024]
                           .rearrange("p (i g) -> p i g", g=8),
                           D=R5[:, 0:16384].rearrange("p (t g) -> p t g", g=8),
                           I=idx: ee.indirect_copy(O, D, I, True),
                           waits=b.wall(), inc="G")
                for c in range(8):
                    cp("vector", R1B[:, 4096 + c * 256:4096 + c * 256 + 256],
                       R1B[:, 0:2048].rearrange("p (i g) -> p i g", g=8)[:, :, c])
                    cp("gpsimd", R1B[:, 6144 + c * 256:6144 + c * 256 + 256],
                       R1B[:, 2048:4096].rearrange("p (i g) -> p i g", g=8)[:, :, c])
                b.dma("scalar", buckh_sp[bi], R1B[:, 4096:6144])
                b.dma("scalar", buckl_sp[bi], R1B[:, 6144:8192])
            _gate(103)
            for e in range(8):
                _gate(120 + e)
                for (kbase, nkb) in ((0, 3), (3, 2)):
                    ycols = nkb * 2048
                    mset(R1A[:, ycols:ycols + 8], 0.0)
                    for kk in range(nkb):
                        bi = e * 5 + kbase + kk
                        b.dma("sync", R4[:, 0:2048], buckh_sp[bi])
                        b.dma("sync", R5[:, 0:2048], buckl_sp[bi])
                        for jq in range(8):
                            for p8 in range(8):
                                st = R1A[:, 6656:7680]
                                b.dma("sync", st,
                                      w1m_d[e][:, p8, 4 * jq:4 * jq + 4, :]
                                      .rearrange("p j u -> p (j u)"))
                                split_sb(st, R4[:, 2048 + p8 * 1024:2048 + p8 * 1024 + 1024],
                                         R5[:, 2048 + p8 * 1024:2048 + p8 * 1024 + 1024],
                                         lo_eng="gpsimd")
                            for j4 in range(4):
                                jg = jq * 4 + j4
                                pa = PB[4][:, 0:256]
                                pg = PB[5][:, 0:256]
                                with b.relax():
                                    tw = b.wall()
                                    for c in range(8):
                                        o = 2048 + c * 1024 + j4 * 256
                                        mm3(pa, R4[:, o:o + 128], R5[:, o:o + 128],
                                            R4[:, c * 256:c * 256 + 256],
                                            R5[:, c * 256:c * 256 + 256],
                                            start=(c == 0), stop=(c == 7),
                                            waits=tw if c == 0 else ())
                                        mm3(pg, R4[:, o + 128:o + 256],
                                            R5[:, o + 128:o + 256],
                                            R4[:, c * 256:c * 256 + 256],
                                            R5[:, c * 256:c * 256 + 256],
                                            start=(c == 0), stop=(c == 7))
                                sa = R1A[:, 7680:7936]
                                act(sa, pa, AF.Silu)
                                tf = R1A[:, 7936:8192]
                                tt("vector", tf, sa, pg, AL.mult)
                                split_sb(tf, R1B[:, jg * 256:jg * 256 + 256],
                                         R1B[:, 8192 + jg * 256:8192 + jg * 256 + 256],
                                         lo_eng="gpsimd")
                        for cf4 in range(8):
                            for p4 in range(4):
                                st = R1A[:, 6656:7680]
                                b.dma("sync", st, w2m_d[e][:, cf4 * 4 + p4, :])
                                split_sb(st, R4[:, 2048 + p4 * 1024:2048 + p4 * 1024 + 1024],
                                         R5[:, 2048 + p4 * 1024:2048 + p4 * 1024 + 1024],
                                         lo_eng="gpsimd")
                            for p4 in range(4):
                                cf = cf4 * 4 + p4
                                with b.relax():
                                    tw = b.wall()
                                    for dc in range(8):
                                        ypb = PB[dc][:, 0:256]
                                        mm3(ypb,
                                            R4[:, 2048 + p4 * 1024 + dc * 128:2048 + p4 * 1024 + dc * 128 + 128],
                                            R5[:, 2048 + p4 * 1024 + dc * 128:2048 + p4 * 1024 + dc * 128 + 128],
                                            R1B[:, cf * 256:cf * 256 + 256],
                                            R1B[:, 8192 + cf * 256:8192 + cf * 256 + 256],
                                            start=(cf == 0), stop=(cf == 31),
                                            waits=tw if dc == 0 else ())
                        for dc in range(8):
                            ypb = PB[dc][:, 0:256]
                            ovw = R1A[:, kk * 2048:kk * 2048 + 2048] \
                                .rearrange("p (s g) -> p s g", g=8)[:, :, dc]
                            cp("vector", ovw, ypb)
                    _gate(150)
                    base = 256 * kbase
                    n = 256 * nkb
                    for tc in range(16):
                        e1 = R1A[:, 8184:8185]
                        ts1("vector", e1, SC_TM[:, 32 + tc:33 + tc], float(e),
                            AL.is_equal)
                        if base > 0:
                            e2 = R1A[:, 8185:8186]
                            ts1("vector", e2, SC_TM[:, 48 + tc:49 + tc],
                                float(base) - 0.5, AL.is_ge)
                            tt("vector", e1, e1, e2, AL.mult)
                        e3 = R1A[:, 8186:8187]
                        ts1("vector", e3, SC_TM[:, 48 + tc:49 + tc],
                            float(base + n) - 0.5, AL.is_lt)
                        tt("vector", e1, e1, e3, AL.mult)
                        rl8 = R1A[:, 8187:8188]
                        ts2("vector", rl8, SC_TM[:, 48 + tc:49 + tc], -float(base),
                            8.0, AL.add, AL.mult)
                        tt("vector", rl8, rl8, e1, AL.mult)
                        u0 = R1A[:, 8188:8189]
                        ts2("vector", u0, e1, -float(ycols), float(ycols),
                            AL.mult, AL.add)
                        tt("vector", R1A[:, 8160 + tc:8161 + tc], rl8, u0, AL.add)
                    trans(PB[5][0:16, 0:128], R1A[:, 8160:8176])
                    cp("vector", SC_TM2[0:16, :], PB[5][0:16, 0:128])
                    b.dma("gpsimd",
                          bass.AP(dram["grid_sp"].tensor, 0,
                                  [[1024, 16], [128, 8], [1, 16]]),
                          SC_TM2[0:16, :].rearrange("b (h l) -> b h l", l=16))
                    b.dma("sync", SC_IDX[:, 0:128], dram["grid_sp"][0:128, :],
                          transpose=True)
                    b.dma("sync", SC_IDX[16:32, 0:128], SC_IDX[0:16, 0:128])
                    b.dma("sync", SC_IDX[32:64, 0:128], SC_IDX[0:32, 0:128])
                    b.dma("sync", SC_IDX[64:128, 0:128], SC_IDX[0:64, 0:128])
                    for sg in range(7):
                        w = 1024 if sg < 6 else 8
                        split_sb(R1A[:, sg * 1024:sg * 1024 + w],
                                 R2[:, sg * 1024:sg * 1024 + w],
                                 R3[:, sg * 1024:sg * 1024 + w],
                                 lo_eng="gpsimd")
                    _gate(151)
                    for q8 in range(16):
                        if q8 == 1:
                            _gate(152)
                        idx = SC_IDX[:, q8 * 8:q8 * 8 + 8]
                        b.emit("gpsimd",
                               lambda ee, O=R1B[:, 0:1024]
                               .rearrange("p (i g) -> p i g", g=8),
                               D=R2[:, 0:16384].rearrange("p (t g) -> p t g", g=8),
                               I=idx: ee.indirect_copy(O, D, I, True),
                               waits=b.wall(), inc="G")
                        b.emit("gpsimd",
                               lambda ee, O=R1B[:, 1024:2048]
                               .rearrange("p (i g) -> p i g", g=8),
                               D=R3[:, 0:16384].rearrange("p (t g) -> p t g", g=8),
                               I=idx: ee.indirect_copy(O, D, I, True),
                               waits=b.wall(), inc="G")
                        tt("vector", R1A[:, 6656:7680], R1B[:, 0:1024],
                           R1B[:, 1024:2048], AL.add)
                        gslc = g_tok[:, q8 * 128:q8 * 128 + 128, :] \
                            .rearrange("p t g -> p (t g)")
                        b.dma("sync", R1A[:, 7680:8192], gslc[:, 0:512])
                        tt("vector", R1A[:, 7680:8192], R1A[:, 7680:8192],
                           R1A[:, 6656:7168], AL.add)
                        b.dma("scalar", gslc[:, 0:512], R1A[:, 7680:8192])
                        b.dma("sync", R1A[:, 7680:8192], gslc[:, 512:1024])
                        tt("vector", R1A[:, 7680:8192], R1A[:, 7680:8192],
                           R1A[:, 7168:7680], AL.add)
                        b.dma("scalar", gslc[:, 512:1024], R1A[:, 7680:8192])

            _gate(200)
            b.dma("sync", SMR[64:65, 0:2048],
                  dram["mp_ln"].rearrange("(a b) -> a b", a=1))
            for sl in range(4):
                b.dma("sync", R1A[:, 0:4096],
                      g_tok[:, sl * 512:(sl + 1) * 512, :].rearrange("p t g -> p (t g)"))
                mm(PB[6][:, 0:512], ONEF64,
                   SMR[64:65, sl * 512:sl * 512 + 512], True, True)
                mpb = R1A[:, 5632:6144]
                cp("vector", mpb, PB[6][:, 0:512])
                for dc in range(8):
                    xpc = R1A[:, 4096:4608]
                    spc = R1A[:, 4608:5120]
                    b.dma("sync", xpc, x1_sp[sl][:, dc * 512:dc * 512 + 512])
                    b.dma("sync", spc, s_sp[sl][:, dc * 512:dc * 512 + 512])
                    gview = R1A[:, 0:4096].rearrange("p (t g) -> p t g", g=8)[:, :, dc]
                    tg = R1A[:, 6144:6656]
                    tt("vector", tg, gview, mpb, AL.mult)
                    sg = R1A[:, 6656:7168]
                    act(sg, tg, AF.Silu)
                    t2 = R1A[:, 7168:7680]
                    tt("vector", t2, sg, spc, AL.mult)
                    ov = R1A[:, 7680:8192]
                    tt("vector", ov, t2, xpc, AL.add)
                    b.dma("scalar", out_fm[dc][:, sl * 512:sl * 512 + 512], ov)

        except _Done:
            pass

        fw = b.wall()
        for eng in ENGS:
            b.emit(eng, None, waits=fw)

        blk = en(nc.Block())
        b.replay(blk)
    return nc, dram


def _bf16(x):
    import ml_dtypes
    return np.asarray(x, dtype=ml_dtypes.bfloat16)


def _f32(x):
    return np.ascontiguousarray(np.asarray(x, dtype=np.float32))


def prep_inputs(I):
    d = {}
    d["wq_d"] = _f32(I["L0_attn_wq"].reshape(8, 128, 256).transpose(1, 0, 2))
    d["wqc_d"] = _f32(I["L0_attn_wqc"].reshape(2, 128, 896).transpose(1, 0, 2))
    wqr = I["L0_attn_wqr"]
    d["wqr_d"] = _f32(wqr.reshape(8, 128, 128).transpose(1, 0, 2))
    wqr_r = np.concatenate([wqr[:, 64:], wqr[:, :64]], axis=1)
    d["wqr_r"] = _f32(wqr_r.reshape(8, 128, 128).transpose(1, 0, 2))
    d["wkv_d"] = _f32(I["L0_attn_wkv"].reshape(8, 128, 256).transpose(1, 0, 2))
    wkr = I["L0_attn_wkr"]
    d["wkr_d"] = _f32(wkr.reshape(8, 128, 128).transpose(1, 0, 2))
    wkr_r = np.concatenate([wkr[:, 64:], wkr[:, :64]], axis=1)
    d["wkr_r"] = _f32(wkr_r.reshape(8, 128, 128).transpose(1, 0, 2))
    d["wkc_d"] = _f32(I["L0_attn_wkc"].reshape(2, 128, 896).transpose(1, 0, 2))
    d["wv_d"] = _f32(I["L0_attn_wv"].reshape(2, 128, 1024).transpose(1, 0, 2))
    d["wo_d"] = _f32(I["L0_attn_wo"].reshape(8, 128, 1024).transpose(1, 0, 2))
    d["k0_d"] = _f32(I["L0_ffn_kernel"].reshape(8, 128, 1024).transpose(1, 0, 2))
    d["a_d"] = _f32(I["L0_ffn_a"].reshape(8, 128, 16).transpose(1, 0, 2)
                    .reshape(128, 128))
    d["b_mat"] = _f32(I["L0_ffn_b"])
    d["wq1_d"] = _f32(I["L1_attn_Wq"].transpose(1, 0, 2).reshape(8, 128, 8, 128)
                      .transpose(1, 0, 2, 3))
    d["wk1_d"] = _f32(I["L1_attn_Wk"].transpose(1, 0, 2).reshape(8, 128, 8, 128)
                      .transpose(1, 0, 2, 3))
    d["wv1_d"] = _f32(I["L1_attn_Wv"].transpose(1, 0, 2).reshape(1024, 1024)
                      .reshape(8, 128, 1024).transpose(1, 0, 2))
    d["wc2q_d"] = _f32(I["L1_attn_Wc2q"].transpose(1, 0, 2))
    d["wo1_d"] = _f32(I["L1_attn_Wo"].reshape(8, 128, 1024).transpose(1, 0, 2))
    d["w1t_d"] = _f32(I["L1_attn_W1"].T.reshape(32, 128, 1024).transpose(1, 0, 2))
    d["wm_d"] = _f32(I["L1_attn_Wm"].reshape(32, 128, 8).transpose(1, 0, 2))
    d["wqot_d"] = _f32(I["L1_attn_Wqo"].T.reshape(32, 128, 8).transpose(1, 0, 2))
    d["bqo_d"] = _f32(I["L1_attn_bqo"].reshape(32, 128).T)
    d["b1_d"] = _f32(I["L1_attn_b1"].reshape(32, 128).T)
    d["w2_d"] = _f32(I["L1_attn_W2"].reshape(32, 128, 1024).transpose(1, 0, 2))
    d["rw_d"] = _f32(I["L1_ffn_router_w"].reshape(8, 128, 8).transpose(1, 0, 2))
    d["w1m_d"] = _f32(I["L1_ffn_w1"].reshape(8, 8, 128, 2, 32, 128)
                      .transpose(0, 2, 1, 4, 3, 5).reshape(8, 128, 8, 32, 256))
    d["w2m_d"] = _f32(I["L1_ffn_w2"].reshape(8, 32, 128, 1024).transpose(0, 2, 1, 3))

    bia = np.zeros((128, 120), dtype=np.float32)
    col = 0

    def put(arr, w):
        nonlocal col
        bia[:, col:col + w] = np.asarray(arr, dtype=np.float32).reshape(w, 128).T
        col += w

    put(I["L0_attn_bq"], 2)
    put(I["L0_attn_bqc"], 7)
    put(I["L0_attn_bkv"], 2)
    put(I["L0_attn_bkc"], 7)
    put(I["L0_attn_bo"], 8)
    put(I["L0_ffn_bias"], 8)
    put(I["L1_attn_bq"].reshape(8, 128), 8)
    put(I["L1_attn_bk"].reshape(8, 128), 8)
    put(I["L1_attn_bo"], 8)
    put(I["L1_attn_g1"], 8)
    put(I["L1_attn_beta1"], 8)
    put(I["L1_attn_g2"], 8)
    put(I["L1_attn_beta2"], 8)
    bia[:, col:col + 8] = np.arange(8, dtype=np.float32)[None, :]
    col += 8
    bia[:, col:col + 8] = 999.0 - np.arange(8, dtype=np.float32)[None, :]
    col += 8
    put(I["L1_attn_b2"], 8)
    d["biases_d"] = bia

    rows = np.zeros((8, 1024), dtype=np.float32)
    rows[0] = np.asarray(I["L0_attn_bv"], dtype=np.float32)
    rows[1] = np.asarray(I["L1_attn_bv"], dtype=np.float32).reshape(1024)
    bqr = np.asarray(I["L0_attn_bqr"], dtype=np.float32)
    bkr = np.asarray(I["L0_attn_bkr"], dtype=np.float32)
    rows[2, 0:128] = bqr
    rows[2, 128:256] = np.concatenate([bqr[64:], bqr[:64]])
    rows[2, 256:384] = bkr
    rows[2, 384:512] = np.concatenate([bkr[64:], bkr[:64]])
    rows[2, 512:576] = np.asarray(I["L1_attn_bc2q"], dtype=np.float32).reshape(64)
    rows[2, 576:584] = np.asarray(I["L1_attn_bm"], dtype=np.float32)
    rows[2, 584:592] = np.asarray(I["L1_ffn_router_b"], dtype=np.float32)
    d["rows_d"] = rows

    d["identf"] = np.eye(128, dtype=np.float32)
    inv_freq = (1.0 / 10000.0 ** (np.arange(0, 128, 2, dtype=np.float64) / 128.0))
    t = np.arange(2048, dtype=np.float64)
    fr = t[None, :] * inv_freq[:, None]
    cs = np.zeros((128, 4096), dtype=np.float32)
    cs[0:64, 0:2048] = np.cos(fr)
    cs[64:128, 0:2048] = np.cos(fr)
    cs[0:64, 2048:4096] = -np.sin(fr)
    cs[64:128, 2048:4096] = np.sin(fr)
    d["cossin_d"] = cs
    kp = np.arange(128)[:, None, None]
    dd = np.arange(4)[None, :, None]
    qf = np.arange(512)[None, None, :]
    mask = np.where(qf >= 128 * dd + kp, 0.0, -1.0e9).astype(np.float32)
    d["masks_d"] = _bf16(mask)
    iv = np.zeros((128, 16, 3), dtype=np.float32)
    tt = 128 * np.arange(16)[None, :] + np.arange(128)[:, None] + 1
    iv[:, :, 0] = tt // 128
    iv[:, :, 1] = tt % 128
    iv[:, :, 2] = 1.0
    d["invvals_d"] = _bf16(iv)
    d["bcfree_d"] = np.broadcast_to(
        np.arange(512, dtype=np.float32)[None, :], (128, 512)).copy()
    tri = (np.arange(128)[:, None] < np.arange(128)[None, :]).astype(np.float32)
    d["tri128_d"] = _bf16(tri)
    return d


def kernel(**inputs):
    nc, _ = build_program()
    shared = prep_inputs(inputs)
    x = np.asarray(inputs["L0_attn_x"], dtype=np.float32)
    in_maps = []
    for c in range(8):
        m = dict(shared)
        m["x_fm"] = np.ascontiguousarray(
            x[c].T.reshape(8, 128, 2048))
        in_maps.append(m)
    res = run_bass_kernel_spmd(nc, in_maps, core_ids=list(range(8)))
    outs = []
    for c in range(8):
        of = np.asarray(res.results[c]["out_fm"], dtype=np.float32)
        outs.append(of.reshape(1024, 2048).T)
    return np.ascontiguousarray(np.stack(outs, axis=0))


# version 05 (final)
# speedup vs baseline: 1.1010x; 1.1010x over previous
import contextlib
import math
import numpy as np

import concourse.bass as bass
import concourse.mybir as mybir
from concourse.bass_utils import run_bass_kernel_spmd

F32 = mybir.dt.float32
BF16 = mybir.dt.bfloat16
U16 = mybir.dt.uint16
AL = mybir.AluOpType
AF = mybir.ActivationFunctionType
AX = mybir.AxisListType.X

T = 2048
NEG = -1.0e9
PI = math.pi

ENGS = ("sync", "scalar", "vector", "tensor", "gpsimd")
ESEM = {"sync": "dS", "scalar": "S", "vector": "V", "tensor": "T", "gpsimd": "G"}
DSEM = {"sync": "dS", "scalar": "dC", "gpsimd": "dG"}


class Builder:
    def __init__(self, nc, ctx):
        self.nc = nc
        self.q = {e: [] for e in ENGS}
        self.sem = {}
        self.cnt = {}
        for name in ("T", "V", "S", "G", "dS", "dC", "dG"):
            self.sem[name] = ctx.enter_context(nc.semaphore("sem_" + name))
            self.cnt[name] = 0
        self.relaxed = None

    def wall(self):
        return tuple((n, v) for n, v in self.cnt.items() if v > 0)

    @contextlib.contextmanager
    def relax(self):
        prev = self.relaxed
        self.relaxed = self.wall()
        try:
            yield
        finally:
            self.relaxed = prev

    def emit(self, eng, fn, waits=(), inc=None, inc_k=1):
        if self.relaxed is None:
            auto = tuple((n, v) for n, v in self.cnt.items() if v > 0)
        else:
            auto = tuple(self.relaxed)
        w = auto + tuple(waits)
        if inc is not None:
            self.cnt[inc] += inc_k
        self.q[eng].append((w, fn, inc, inc_k))

    def snap(self, *names):
        if len(names) == 1:
            return self.cnt[names[0]]
        return tuple(self.cnt[n] for n in names)

    def dma(self, eng, out, in_, waits=(), **kw):
        sem_name = DSEM[eng]
        self.emit(eng, lambda e, o=out, i=in_, k=dict(kw): e.dma_start(out=o, in_=i, **k),
                  waits=waits, inc=sem_name, inc_k=16)
        return self.cnt[sem_name]

    def replay(self, block):
        def make_runner(engname):
            items = self.q[engname]

            def run(eng):
                wd = {}
                for (w, fn, inc, k) in items:
                    for (s, v) in w:
                        if wd.get(s, -1) < v:
                            eng.wait_ge(self.sem[s], v)
                            wd[s] = v
                    if fn is None:
                        continue
                    inst = fn(eng)
                    if inc is not None:
                        inst.then_inc(self.sem[inc], k)
            return run

        block.sync(make_runner("sync"))
        block.scalar(make_runner("scalar"))
        block.vector(make_runner("vector"))
        block.tensor(make_runner("tensor"))
        block.gpsimd(make_runner("gpsimd"))


class _Done(Exception):
    pass


def build_program(taps=(), upto=10 ** 9):
    nc = bass.Bass("TRN2", target_bir_lowering=False)
    dram = {}

    def din(name, shape, dtype=F32):
        dram[name] = nc.dram_tensor(name, shape, dtype, kind="ExternalInput").ap()
        return dram[name]

    def dscr(name, shape, dtype=F32):
        kind = "ExternalOutput" if name in taps else "Internal"
        dram[name] = nc.dram_tensor(name, shape, dtype, kind=kind).ap()
        return dram[name]

    x_fm = din("x_fm", [8, 128, 2048])
    wq_d = din("wq_d", [128, 8, 256])
    wqc_d = din("wqc_d", [128, 2, 896])
    wqr_d = din("wqr_d", [128, 8, 128])
    wqr_r = din("wqr_r", [128, 8, 128])
    wkv_d = din("wkv_d", [128, 8, 256])
    wkr_d = din("wkr_d", [128, 8, 128])
    wkr_r = din("wkr_r", [128, 8, 128])
    wkc_d = din("wkc_d", [128, 2, 896])
    wv_d = din("wv_d", [128, 2, 1024])
    wo_d = din("wo_d", [128, 8, 1024])
    biases_d = din("biases_d", [128, 120])
    rows_d = din("rows_d", [8, 1024])
    k0_d = din("k0_d", [128, 8, 1024])
    a_d = din("a_d", [128, 128])
    b_mat = din("b_mat", [16, 1024])
    wq1_d = din("wq1_d", [128, 8, 8, 128])
    wk1_d = din("wk1_d", [128, 8, 8, 128])
    wv1_d = din("wv1_d", [128, 8, 1024])
    wc2q_d = din("wc2q_d", [128, 8, 8])
    wo1_d = din("wo1_d", [128, 8, 1024])
    w1t_d = din("w1t_d", [128, 32, 1024])
    wm_d = din("wm_d", [128, 32, 8])
    wqot_d = din("wqot_d", [128, 32, 8])
    bqo_d = din("bqo_d", [128, 32])
    b1_d = din("b1_d", [128, 32])
    w2_d = din("w2_d", [128, 32, 1024])
    rw_d = din("rw_d", [128, 8, 8])
    w1m_d = din("w1m_d", [8, 128, 8, 32, 256])
    w2m_d = din("w2m_d", [8, 128, 32, 1024])
    identf = din("identf", [128, 128])
    cossin_d = din("cossin_d", [128, 4096])
    masks_d = din("masks_d", [128, 4, 512], BF16)
    invvals_d = din("invvals_d", [128, 16, 3], BF16)
    bcfree_d = din("bcfree_d", [128, 512])
    tri128_d = din("tri128_d", [128, 128], BF16)

    g0_sp = dscr("g0_sp", [4, 128, 4096])
    q0h_sp = dscr("q0h_sp", [4, 128, 4096], BF16)
    q0l_sp = dscr("q0l_sp", [4, 128, 4096], BF16)
    k0h_sp = dscr("k0h_sp", [4, 128, 4096], BF16)
    k0l_sp = dscr("k0l_sp", [4, 128, 4096], BF16)
    v0h_sp = dscr("v0h_sp", [16, 128, 1040], BF16)
    v0l_sp = dscr("v0l_sp", [16, 128, 1040], BF16)
    pvh_sp = dscr("pvh_sp", [16, 4, 128, 512], BF16)
    pvl_sp = dscr("pvl_sp", [16, 4, 128, 512], BF16)
    x1_sp = dscr("x1_sp", [4, 128, 4096])
    s_sp = dscr("s_sp", [4, 128, 4096])
    q1h_sp = dscr("q1h_sp", [8, 128, 2048], BF16)
    q1l_sp = dscr("q1l_sp", [8, 128, 2048], BF16)
    v1h_sp = dscr("v1h_sp", [16, 128, 1024], BF16)
    v1l_sp = dscr("v1l_sp", [16, 128, 1024], BF16)
    kq_sp = dscr("kq_sp", [64, 2048])
    qq_sp = dscr("qq_sp", [64, 2048])
    mp_ln = dscr("mp_ln", [2048])
    g_tok = dscr("g_tok", [128, 2048, 8])
    wf1_sp = dscr("wf1_sp", [128, 64])
    wf2_sp = dscr("wf2_sp", [9, 1024])
    buckh_sp = dscr("buckh_sp", [40, 128, 2048], BF16)
    buckl_sp = dscr("buckl_sp", [40, 128, 2048], BF16)
    grid_sp = dscr("grid_sp", [640, 128], U16)
    lg_sp = dscr("lg_sp", [128, 128])

    xn1h_tap = dscr("xn1h_tap", [128, 16384], BF16) if "xn1h_tap" in taps else None
    xn1l_tap = dscr("xn1l_tap", [128, 16384], BF16) if "xn1l_tap" in taps else None

    out_fm = nc.dram_tensor("out_fm", [8, 128, 2048], F32, kind="ExternalOutput").ap()

    with contextlib.ExitStack() as ctx:
        en = ctx.enter_context
        R1A = en(nc.sbuf_tensor("R1A", [128, 8192], F32))
        R1B = en(nc.sbuf_tensor("R1B", [128, 16384], BF16))
        R2 = en(nc.sbuf_tensor("R2", [128, 16640], BF16))
        R3 = en(nc.sbuf_tensor("R3", [128, 16640], BF16))
        R4 = en(nc.sbuf_tensor("R4", [128, 16384], BF16))
        R5 = en(nc.sbuf_tensor("R5", [128, 16384], BF16))
        SC_ID = en(nc.sbuf_tensor("SC_ID", [128, 128], F32))
        SC_B = en(nc.sbuf_tensor("SC_B", [128, 120], F32))
        SMR = en(nc.sbuf_tensor("SMR", [128, 2048], F32))
        SC_IDX = en(nc.sbuf_tensor("SC_IDX", [128, 640], U16))
        SC_TM = en(nc.sbuf_tensor("SC_TM", [128, 80], F32))
        SC_TM2 = en(nc.sbuf_tensor("SC_TM2", [128, 128], F32))
        SC_ONEC = en(nc.sbuf_tensor("SC_ONEC", [128, 1], BF16))
        SC_ONEF = en(nc.sbuf_tensor("SC_ONEF", [1, 128], F32))
        ONEF64T = en(nc.sbuf_tensor("ONEF64T", [65, 128], F32))
        ONEF64 = ONEF64T[64:65, :]
        SC_CST = en(nc.sbuf_tensor("SC_CST", [128, 4], F32))
        CONSTS = (1e-5, 0.125, 0.5, PI)
        for _k, _v in enumerate(CONSTS):
            nc.const_aps.register(F32, _v, SC_CST[:, _k:_k + 1])
        PB = [en(nc.psum_tensor("pb%d" % i, [128, 512], F32)) for i in range(8)]

        b = Builder(nc, ctx)

        def _gate(n):
            if upto < n:
                raise _Done()

        def mm(psum_ap, lhs, rhs, start, stop, waits=()):
            b.emit("tensor",
                   lambda e, p=psum_ap, l=lhs, r=rhs, s1=start, s2=stop:
                   e.matmul(p, l, r, start=s1, stop=s2), waits=waits, inc="T")
            return b.snap("T")

        def mm3(psum_ap, lh, ll, rh, rl, start, stop, waits=()):
            mm(psum_ap, lh, rh, start, False, waits=waits)
            mm(psum_ap, lh, rl, False, False)
            return mm(psum_ap, ll, rh, False, stop)

        def mm2r(psum_ap, lhs_exact, rh, rl, start, stop, waits=()):
            mm(psum_ap, lhs_exact, rh, start, False, waits=waits)
            return mm(psum_ap, lhs_exact, rl, False, stop)

        def mm2l(psum_ap, lh, ll, rhs_exact, start, stop, waits=()):
            mm(psum_ap, lh, rhs_exact, start, False, waits=waits)
            return mm(psum_ap, ll, rhs_exact, False, stop)

        def trans(psum_ap, in_ap, waits=()):
            k = in_ap.shape[0]
            b.emit("tensor", lambda e, p=psum_ap, i=in_ap, n=k:
                   e.transpose(p, i, SC_ID[0:n, 0:n]), waits=waits, inc="T")
            return b.snap("T")

        def act(out_ap, in_ap, func, bias=0.0, scale=1.0, waits=()):
            b.emit("scalar",
                   lambda e, o=out_ap, i=in_ap, f=func, ba=bias, sc=scale:
                   e.activation(o, i, f, bias=ba, scale=sc), waits=waits, inc="S")
            return b.snap("S")

        def vop(fn, waits=(), eng="vector"):
            b.emit(eng, fn, waits=waits, inc=ESEM[eng])
            return b.snap(ESEM[eng])

        def tt(eng, o, x, y, op, waits=()):
            return vop(lambda e, O=o, X=x, Y=y, P=op: e.tensor_tensor(O, X, Y, P),
                       waits, eng)

        def ts2(eng, o, x, s1, s2, op0, op1, waits=()):
            return vop(lambda e, O=o, X=x, A=s1, C=s2, P0=op0, P1=op1:
                       e.tensor_scalar(O, X, A, C, P0, P1), waits, eng)

        def ts1(eng, o, x, s1, op, waits=()):
            return vop(lambda e, O=o, X=x, A=s1, P=op:
                       e.tensor_single_scalar(O, X, A, P), waits, eng)

        def stt(eng, o, x, s, y, op0, op1, waits=()):
            return vop(lambda e, O=o, X=x, A=s, Y=y, P0=op0, P1=op1:
                       e.scalar_tensor_tensor(O, X, A, Y, P0, P1), waits, eng)

        def cp(eng, o, x, waits=()):
            return vop(lambda e, O=o, X=x: e.tensor_copy(O, X), waits, eng)

        def recip(o, x, waits=()):
            return vop(lambda e, O=o, X=x: e.reciprocal(O, X), waits, "vector")

        def red(o, x, op, absval=False, negate=False, waits=()):
            return vop(lambda e, O=o, X=x, P=op, av=absval, ng=negate:
                       e.tensor_reduce(O, X, AX, P, apply_absolute_value=av,
                                       negate=ng), waits, "vector")

        def mset(o, val, waits=(), eng="vector"):
            return vop(lambda e, O=o, V=val: e.memset(O, V), waits, eng)

        def split_ps(psum_ap, hi_ap, lo_ap, scale=None, bias_ap=None, waits=()):
            if scale is not None:
                act(hi_ap, psum_ap, AF.Copy, scale=float(scale), waits=waits)
                stt("vector", lo_ap, psum_ap, float(scale), hi_ap,
                    AL.mult, AL.subtract)
            elif bias_ap is not None:
                u = R1A[:, 2560:3072]
                ts1("vector", u, psum_ap, bias_ap, AL.add, waits=waits)
                act(hi_ap, u, AF.Copy)
                tt("gpsimd", lo_ap, u, hi_ap, AL.subtract)
            else:
                act(hi_ap, psum_ap, AF.Copy, waits=waits)
                stt("vector", lo_ap, psum_ap, 0.0, hi_ap, AL.add, AL.subtract)
            return b.snap("S", "V")

        def split_sb(src_ap, hi_ap, lo_ap, waits=(), lo_eng="vector"):
            act(hi_ap, src_ap, AF.Copy, waits=waits)
            tt(lo_eng, lo_ap, src_ap, hi_ap, AL.subtract)
            return b.snap("S", ESEM[lo_eng])

        BOFF = {}
        boff = 0
        for nm, w in [("bq", 2), ("bqc", 7), ("bkv", 2), ("bkc", 7), ("bo", 8),
                      ("fb", 8), ("bq1", 8), ("bk1", 8), ("bo1", 8), ("g1", 8),
                      ("be1", 8), ("g2", 8), ("be2", 8), ("eidx", 8),
                      ("c999", 8), ("b2e", 8)]:
            BOFF[nm] = boff
            boff += w
        assert boff <= 128

        def bias(nm, c=0):
            return SC_B[:, BOFF[nm] + c:BOFF[nm] + c + 1]

        ONES512F = SMR[32:33, 0:512]
        RR = {"bqr": SMR[32:33, 512:640], "bqr_r": SMR[32:33, 640:768],
              "bkr": SMR[32:33, 768:896], "bkr_r": SMR[32:33, 896:1024],
              "bc2q": SMR[32:33, 1024:1088], "bm": SMR[32:33, 1088:1096],
              "rb": SMR[32:33, 1096:1104]}
        ST_MU = SMR[64:65, 0:512]
        ST_RS = SMR[64:65, 512:1024]
        ST_TA = SMR[64:65, 1024:1536]
        ST_TB = SMR[64:65, 1536:2048]

        try:
            b.dma("sync", SC_ID[:], identf)
            b.dma("sync", SC_B[:], biases_d)
            b.dma("sync", SMR[0:1, 0:1024], rows_d[0:1, :])
            b.dma("sync", SMR[0:1, 1024:2048], rows_d[1:2, :])
            b.dma("sync", SMR[32:33, 512:1104], rows_d[2:3, 0:592])
            mset(SC_ONEC[:], 1.0)
            for _k, _v in enumerate(CONSTS):
                mset(SC_CST[:, _k:_k + 1], _v)
            mset(SC_ONEF[:], 1.0)
            mset(ONES512F, 1.0)
            mset(ONEF64, 1.0)

            _gate(1)
            xa = R1A[:, 6144:8192]
            for c in range(8):
                b.dma("sync", xa, x_fm[c])
                if c == 0:
                    cp("vector", R1A[:, 0:2048], xa)
                    act(R1A[:, 2048:4096], xa, AF.Square)
                else:
                    tt("vector", R1A[:, 0:2048], R1A[:, 0:2048], xa, AL.add)
                    act(R1A[:, 4096:6144], xa, AF.Square)
                    tt("vector", R1A[:, 2048:4096], R1A[:, 2048:4096],
                       R1A[:, 4096:6144], AL.add)
            split_sb(R1A[:, 0:2048], R4[:, 0:2048], R5[:, 0:2048])
            split_sb(R1A[:, 2048:4096], R4[:, 2048:4096], R5[:, 2048:4096])
            for sl in range(4):
                s0, s1 = sl * 512, (sl + 1) * 512
                mm2r(PB[6][0:1, 0:512], SC_ONEC[:], R4[:, s0:s1], R5[:, s0:s1],
                     True, True)
                mm2r(PB[7][0:1, 0:512], SC_ONEC[:], R4[:, 2048 + s0:2048 + s1],
                     R5[:, 2048 + s0:2048 + s1], True, True)
                mu = ST_MU
                rstd = ST_RS
                ts1("vector", mu, PB[6][0:1, 0:512], 1.0 / 1024.0, AL.mult)
                ts1("vector", ST_TB, PB[7][0:1, 0:512], 1.0 / 1024.0, AL.mult)
                tt("vector", ST_TA, mu, mu, AL.mult)
                tt("vector", ST_TA, ST_TB, ST_TA, AL.subtract)
                act(ST_TB, ST_TA, AF.Sqrt, bias=1e-5)
                recip(rstd, ST_TB)
                stt("vector", ST_TA, mu, -1.0, rstd, AL.mult, AL.mult)
                mm(PB[6][:, 0:512], ONEF64, rstd, True, True)
                mm(PB[7][:, 0:512], ONEF64, ST_TA, True, True)
                for c in range(8):
                    xp = R1A[:, 4096 + (c % 2) * 512:4096 + (c % 2) * 512 + 512]
                    b.dma("sync", xp, x_fm[c][:, s0:s1])
                    u = R1A[:, 5120:5632]
                    tt("vector", u, xp, PB[6][:, 0:512], AL.mult)
                    tt("vector", u, u, PB[7][:, 0:512], AL.add)
                    split_sb(u, R2[:, c * 2048 + s0:c * 2048 + s1],
                             R3[:, c * 2048 + s0:c * 2048 + s1], lo_eng="gpsimd")

            _gate(2)
            for pc in range(8):
                piece = R1A[:, 0:1024]
                b.dma("sync", piece, k0_d[:, pc, :])
                red(SC_TM[:, 8 + pc:9 + pc], piece, AL.max, absval=True)
            red(SC_TM[:, 0:1], SC_TM[:, 8:16], AL.max)
            mm(PB[0][0:1, 0:128], SC_TM[:, 0:1], SC_ID[:], True, True)
            red(SMR[64:65, 1024:1025], PB[0][0:1, 0:128], AL.max)
            ts1("vector", SMR[64:65, 1025:1026], SMR[64:65, 1024:1025], 1.0 / 7.0,
                AL.mult)
            recip(SMR[64:65, 1026:1027], SMR[64:65, 1025:1026])
            mm(PB[1][:, 0:2], ONEF64, SMR[64:65, 1025:1027], True, True)
            cp("vector", SC_TM[:, 1:3], PB[1][:, 0:2])
            scale_pp = SC_TM[:, 1:2]
            invs_pp = SC_TM[:, 2:3]
            for pc in range(8):
                piece = R1A[:, 0:1024]
                ws = R1A[:, 1024:2048]
                a1 = R1A[:, 2048:3072]
                a2 = R1A[:, 3072:4096]
                b.dma("sync", piece, k0_d[:, pc, :])
                ts1("vector", ws, piece, invs_pp, AL.mult)
                mset(a1, -7.0)
                cur, nxt = a1, a2
                for k in range(-7, 7):
                    eng = "vector" if k % 2 == 0 else "vector"
                    stt(eng, nxt, ws, float(k) + 0.5, cur, AL.is_ge, AL.add)
                    cur, nxt = nxt, cur
                ts2("vector", cur, cur, 7.0, -8.0, AL.min, AL.max)
                cp("vector", R4[:, pc * 1024:(pc + 1) * 1024], cur)

            b.dma("sync", R1A[:, 6144:6272], a_d)
            split_sb(R1A[:, 6144:6272], R5[:, 0:128], R5[:, 128:256])
            b.dma("sync", R1A[0:16, 4096:5120], b_mat)
            ts1("vector", R1A[0:16, 5120:6144], R1A[0:16, 4096:5120],
                SC_TM[0:16, 2:3], AL.mult)
            bp_h = R5[0:16, 256:1280]
            bp_l = R5[0:16, 1280:2304]
            split_sb(R1A[0:16, 5120:6144], bp_h, bp_l)
            t1h = R5[0:16, 2304:2816]
            t1l = R5[0:16, 2816:3328]

            for sl in range(4):
                s0, s1 = sl * 512, (sl + 1) * 512
                pt1 = PB[2][0:16, 0:512]
                for c in range(8):
                    mm3(pt1, R5[:, 0:128][:, c * 16:c * 16 + 16],
                        R5[:, 128:256][:, c * 16:c * 16 + 16],
                        R2[:, c * 2048 + s0:c * 2048 + s1],
                        R3[:, c * 2048 + s0:c * 2048 + s1],
                        start=(c == 0), stop=(c == 7))
                split_ps(pt1, t1h, t1l)
                for dc in range(8):
                    pb = PB[dc % 2][:, 0:512]
                    with b.relax():
                        tw = b.wall()
                        for c in range(8):
                            lhs = R4[:, c * 1024 + dc * 128:c * 1024 + dc * 128 + 128]
                            mm(pb, lhs, R2[:, c * 2048 + s0:c * 2048 + s1],
                               start=(c == 0), stop=False, waits=tw if c == 0 else ())
                            mm(pb, lhs, R3[:, c * 2048 + s0:c * 2048 + s1],
                               start=False, stop=False)
                        mm3(pb, bp_h[:, dc * 128:dc * 128 + 128],
                            bp_l[:, dc * 128:dc * 128 + 128], t1h, t1l,
                            start=False, stop=True)
                    gsl = R1A[:, 5120 + (dc % 2) * 512:5120 + (dc % 2) * 512 + 512]
                    act(gsl, pb, AF.Silu, bias=bias("fb", dc), scale=scale_pp)
                    b.dma("scalar", g0_sp[sl][:, dc * 512:dc * 512 + 512], gsl)

            _gate(3)
            def load_wsplit(dst_off, n, src_ap):
                st = R1A[:, 6144:6144 + n]
                b.dma("sync", st, src_ap)
                split_sb(st, R4[:, dst_off:dst_off + n], R5[:, dst_off:dst_off + n],
                         lo_eng="gpsimd")

            load_wsplit(8192, 1024, wqr_d.rearrange("p c f -> p (c f)"))
            load_wsplit(9216, 1024, wqr_r.rearrange("p c f -> p (c f)"))
            load_wsplit(10240, 1024, wkr_d.rearrange("p c f -> p (c f)"))
            load_wsplit(11264, 1024, wkr_r.rearrange("p c f -> p (c f)"))
            b.dma("sync", R1A[:, 0:4096], cossin_d)
            cosb = R1A[:, 0:2048]
            sinb = R1A[:, 2048:4096]
            for side in range(2):
                woff = 8192 if side == 0 else 10240
                woff_r = woff + 1024
                brow = "bqr" if side == 0 else "bkr"
                brow_r = brow + "_r"
                sph = q0h_sp if side == 0 else k0h_sp
                spl = q0l_sp if side == 0 else k0l_sp
                for sl in range(4):
                    s0, s1 = sl * 512, (sl + 1) * 512
                    p0 = PB[2][:, 0:512]
                    p1 = PB[3][:, 0:512]
                    for c in range(8):
                        mm3(p0, R4[:, woff + c * 128:woff + c * 128 + 128],
                            R5[:, woff + c * 128:woff + c * 128 + 128],
                            R2[:, c * 2048 + s0:c * 2048 + s1],
                            R3[:, c * 2048 + s0:c * 2048 + s1],
                            start=(c == 0), stop=False)
                    mm(p0, RR[brow], ONES512F, False, True)
                    for c in range(8):
                        mm3(p1, R4[:, woff_r + c * 128:woff_r + c * 128 + 128],
                            R5[:, woff_r + c * 128:woff_r + c * 128 + 128],
                            R2[:, c * 2048 + s0:c * 2048 + s1],
                            R3[:, c * 2048 + s0:c * 2048 + s1],
                            start=(c == 0), stop=False)
                    mm(p1, RR[brow_r], ONES512F, False, True)
                    u = R1A[:, 4096:4608]
                    u2 = R1A[:, 4608:5120]
                    tt("vector", u, p0, cosb[:, s0:s1], AL.mult)
                    tt("vector", u2, p1, sinb[:, s0:s1], AL.mult)
                    tt("vector", u, u, u2, AL.add)
                    g = R1A[:, 5120:5632]
                    act(g, u, AF.Gelu)
                    hi = R1B[:, 0:512]
                    lo = R1B[:, 512:1024]
                    if side == 0:
                        split_sb(g, hi, lo, lo_eng="gpsimd")
                    else:
                        act(hi, g, AF.Copy, scale=0.125)
                        stt("vector", lo, g, 0.125, hi, AL.mult, AL.subtract)
                    b.dma("scalar", sph[sl][:, 3584:4096], hi)
                    b.dma("scalar", spl[sl][:, 3584:4096], lo)

            _gate(4)
            load_wsplit(8192, 2048, wq_d.rearrange("p c f -> p (c f)"))
            load_wsplit(10240, 896, wqc_d[:, 0, :])
            load_wsplit(11136, 896, wqc_d[:, 1, :])
            load_wsplit(12032, 2048, wkv_d.rearrange("p c f -> p (c f)"))
            load_wsplit(14080, 896, wkc_d[:, 0, :])
            load_wsplit(14976, 896, wkc_d[:, 1, :])
            for sl in range(4):
                s0, s1 = sl * 512, (sl + 1) * 512
                for side in range(2):
                    woff = 8192 if side == 0 else 12032
                    wcoff = 10240 if side == 0 else 14080
                    bnm = "bq" if side == 0 else "bkv"
                    bcnm = "bqc" if side == 0 else "bkc"
                    sph = q0h_sp if side == 0 else k0h_sp
                    spl = q0l_sp if side == 0 else k0l_sp
                    for c2 in range(2):
                        p = PB[2][:, 0:512]
                        for c in range(8):
                            o = woff + c * 256 + c2 * 128
                            mm3(p, R4[:, o:o + 128], R5[:, o:o + 128],
                                R2[:, c * 2048 + s0:c * 2048 + s1],
                                R3[:, c * 2048 + s0:c * 2048 + s1],
                                start=(c == 0), stop=(c == 7))
                        u = R1A[:, 4096:4608]
                        act(u, p, AF.Gelu, bias=bias(bnm, c2))
                        if side == 0:
                            split_sb(u, R1B[:, c2 * 512:c2 * 512 + 512],
                                     R1B[:, 1024 + c2 * 512:1024 + c2 * 512 + 512],
                                     lo_eng="gpsimd")
                        else:
                            split_sb(u, R4[:, 4096 + c2 * 2048 + s0:4096 + c2 * 2048 + s1],
                                     R5[:, 4096 + c2 * 2048 + s0:4096 + c2 * 2048 + s1],
                                     lo_eng="gpsimd")
                    for f7 in range(7):
                        p = PB[3][:, 0:512]
                        for c2 in range(2):
                            if side == 0:
                                rh = R1B[:, c2 * 512:c2 * 512 + 512]
                                rl = R1B[:, 1024 + c2 * 512:1024 + c2 * 512 + 512]
                            else:
                                rh = R4[:, 4096 + c2 * 2048 + s0:4096 + c2 * 2048 + s1]
                                rl = R5[:, 4096 + c2 * 2048 + s0:4096 + c2 * 2048 + s1]
                            o = wcoff + c2 * 896 + f7 * 128
                            mm3(p, R4[:, o:o + 128], R5[:, o:o + 128], rh, rl,
                                start=(c2 == 0), stop=(c2 == 1))
                        u = R1A[:, 4608:5120]
                        act(u, p, AF.Gelu, bias=bias(bcnm, f7))
                        hi = R1B[:, 2048:2560]
                        lo = R1B[:, 2560:3072]
                        if side == 0:
                            split_sb(u, hi, lo, lo_eng="gpsimd")
                        else:
                            act(hi, u, AF.Copy, scale=0.125)
                            stt("vector", lo, u, 0.125, hi, AL.mult, AL.subtract)
                        b.dma("scalar", sph[sl][:, f7 * 512:f7 * 512 + 512], hi)
                        b.dma("scalar", spl[sl][:, f7 * 512:f7 * 512 + 512], lo)

            _gate(5)
            load_wsplit(0, 2048, wv_d.rearrange("p c f -> p (c f)"))
            vview_h = R1B[:, 0:1040].rearrange("p (h w) -> p h w", w=65)
            vview_l = R1B[:, 1040:2080].rearrange("p (h w) -> p h w", w=65)
            mset(vview_h[:, :, 64:65].rearrange("p h w -> p (h w)"), 1.0)
            mset(vview_l[:, :, 64:65].rearrange("p h w -> p (h w)"), 0.0)
            for tgt in range(16):
                for hh in range(2):
                    p = PB[2][:, 0:512]
                    for c2 in range(2):
                        mm3(p, R4[:, 4096 + c2 * 2048 + tgt * 128:4096 + c2 * 2048 + tgt * 128 + 128],
                            R5[:, 4096 + c2 * 2048 + tgt * 128:4096 + c2 * 2048 + tgt * 128 + 128],
                            R4[:, 0 + c2 * 1024 + hh * 512:0 + c2 * 1024 + hh * 512 + 512],
                            R5[:, 0 + c2 * 1024 + hh * 512:0 + c2 * 1024 + hh * 512 + 512],
                            start=(c2 == 0), stop=False)
                    mm(p, SC_ONEF[0:1, 0:128], SMR[0:1, hh * 512:hh * 512 + 512],
                       False, True)
                    oh = vview_h[:, hh * 8:hh * 8 + 8, 0:64]
                    ol = vview_l[:, hh * 8:hh * 8 + 8, 0:64]
                    split_ps(p, oh, ol)
                b.dma("scalar", v0h_sp[tgt], R1B[:, 0:1040])
                b.dma("scalar", v0l_sp[tgt], R1B[:, 1040:2080])
            for hp in range(8):
                st = R1A[:, 6144:7168]
                b.dma("sync", st, wo_d[:, hp, :])
                split_sb(st, R4[:, 8192 + hp * 1024:8192 + hp * 1024 + 1024],
                         R5[:, 8192 + hp * 1024:8192 + hp * 1024 + 1024],
                         lo_eng="gpsimd")

            _gate(6)
            for tgt in range(16):
                b.dma("sync", R2[:, tgt * 1040:(tgt + 1) * 1040], v0h_sp[tgt])
                b.dma("sync", R3[:, tgt * 1040:(tgt + 1) * 1040], v0l_sp[tgt])
            b.dma("sync", R4[:, 0:2048], masks_d.rearrange("p d q -> p (d q)"))
            MK = R4[:, 0:2048].rearrange("p (d q) -> p d q", q=512)
            for h in range(16):
                band = 64 * (h % 2)
                chn = h // 2
                for sl in range(4):
                    b.dma("sync", R1B[0:64, sl * 512:(sl + 1) * 512],
                          k0h_sp[sl][band:band + 64, chn * 512:(chn + 1) * 512])
                    b.dma("sync", R1B[0:64, 2048 + sl * 512:2048 + (sl + 1) * 512],
                          k0l_sp[sl][band:band + 64, chn * 512:(chn + 1) * 512])
                    b.dma("sync", R1B[0:64, 4096 + sl * 512:4096 + (sl + 1) * 512],
                          q0h_sp[sl][band:band + 64, chn * 512:(chn + 1) * 512])
                    b.dma("sync", R1B[0:64, 6144 + sl * 512:6144 + (sl + 1) * 512],
                          q0l_sp[sl][band:band + 64, chn * 512:(chn + 1) * 512])
                KHh = R1B[0:64, 0:2048]
                KHl = R1B[0:64, 2048:4096]
                QHh = R1B[0:64, 4096:6144]
                QHl = R1B[0:64, 6144:8192]
                for j in range(16):
                    i_lo = j // 4
                    sc_p = PB[0 if j % 2 == 0 else 1][:, 0:512]
                    for i in range(i_lo, 4):
                        with b.relax():
                            mm3(sc_p, KHh[:, j * 128:(j + 1) * 128],
                                KHl[:, j * 128:(j + 1) * 128],
                                QHh[:, i * 512:(i + 1) * 512],
                                QHl[:, i * 512:(i + 1) * 512],
                                start=True, stop=True)
                        Pt = R1A[:, 2048:2560]
                        if j // 4 == i:
                            ein = R1A[:, 0:512]
                            tt("vector", ein, sc_p, MK[:, j % 4, :], AL.add)
                            act(Pt, ein, AF.Exp)
                        else:
                            act(Pt, sc_p, AF.Exp)
                        Phi = R1B[:, 8192:8704]
                        Plo = R1B[:, 8704:9216]
                        act(Phi, Pt, AF.Copy)
                        tt("gpsimd", Plo, Pt, Phi, AL.subtract)
                        pv = PB[2 + i]
                        vh = R2[:, j * 1040 + h * 65:j * 1040 + h * 65 + 65]
                        vl = R3[:, j * 1040 + h * 65:j * 1040 + h * 65 + 65]
                        with b.relax():
                            tw = b.wall()
                            mm(pv[0:65, 0:512], vh, Phi, start=(j == 0),
                               stop=False, waits=tw)
                            mm(pv[0:65, 0:512], vh, Plo, start=False, stop=False)
                            mm(pv[0:65, 0:512], vl, Phi, start=False,
                               stop=(j == 4 * i + 3))
                for i in range(4):
                    pv = PB[2 + i]
                    nrow = ST_MU
                    cp("vector", nrow, pv[64:65, 0:512])
                    rn = ST_RS
                    recip(rn, nrow)
                    mm(PB[6][0:64, 0:512], ONEF64[0:1, 0:64], rn, True, True)
                    rnf = R1A[:, 3072:3584]
                    cp("vector", rnf[0:64, :], PB[6][0:64, 0:512])
                    pvn = R1A[:, 3584:4096]
                    tt("vector", pvn[0:64, :], pv[0:64, 0:512], rnf[0:64, :], AL.mult)
                    hi = R1B[:, 9216:9728]
                    lo = R1B[:, 9728:10240]
                    split_sb(pvn[0:64, :], hi[0:64, :], lo[0:64, :])
                    b.dma("scalar", pvh_sp[h][i][0:64, :], hi[0:64, :])
                    b.dma("scalar", pvl_sp[h][i][0:64, :], lo[0:64, :])
            for i in range(4):
                for hp in range(8):
                    b.dma("sync", R1B[0:64, hp * 512:(hp + 1) * 512],
                          pvh_sp[2 * hp][i][0:64, :])
                    b.dma("sync", R1B[64:128, hp * 512:(hp + 1) * 512],
                          pvh_sp[2 * hp + 1][i][0:64, :])
                    b.dma("sync", R1B[0:64, 8192 + hp * 512:8192 + (hp + 1) * 512],
                          pvl_sp[2 * hp][i][0:64, :])
                    b.dma("sync", R1B[64:128, 8192 + hp * 512:8192 + (hp + 1) * 512],
                          pvl_sp[2 * hp + 1][i][0:64, :])
                for dc in range(8):
                    p = PB[dc % 2][:, 0:512]
                    with b.relax():
                        tw = b.wall()
                        for hp in range(8):
                            mm3(p, R4[:, 8192 + hp * 1024 + dc * 128:8192 + hp * 1024 + dc * 128 + 128],
                                R5[:, 8192 + hp * 1024 + dc * 128:8192 + hp * 1024 + dc * 128 + 128],
                                R1B[:, hp * 512:(hp + 1) * 512],
                                R1B[:, 8192 + hp * 512:8192 + (hp + 1) * 512],
                                start=(hp == 0), stop=(hp == 7),
                                waits=tw if hp == 0 else ())
                    s0b = R1A[:, 2048:2560]
                    ts1("vector", s0b, p, bias("bo", dc), AL.add)
                    xpc = R1A[:, 2560:3072]
                    gpc = R1A[:, 3072:3584]
                    b.dma("sync", xpc, x_fm[dc][:, i * 512:(i + 1) * 512])
                    b.dma("sync", gpc, g0_sp[i][:, dc * 512:dc * 512 + 512])
                    t2 = R1A[:, 3584:4096]
                    tt("vector", t2, s0b, gpc, AL.mult)
                    x1p = R1A[:, 4096:4608]
                    tt("vector", x1p, xpc, t2, AL.add)
                    b.dma("scalar", x1_sp[i][:, dc * 512:dc * 512 + 512], x1p)

            _gate(7)
            for sl in range(4):
                b.dma("sync", R1A[:, 0:4096], x1_sp[sl])
                acc = R1A[:, 4096:4608]
                acq = R1A[:, 4608:5120]
                sq = R1A[:, 5120:5632]
                cp("vector", acc, R1A[:, 0:512])
                act(acq, R1A[:, 0:512], AF.Square)
                for c in range(1, 8):
                    tt("vector", acc, acc, R1A[:, c * 512:(c + 1) * 512], AL.add)
                    act(sq, R1A[:, c * 512:(c + 1) * 512], AF.Square)
                    tt("vector", acq, acq, sq, AL.add)
                split_sb(acc, R1B[:, 0:512], R1B[:, 512:1024])
                split_sb(acq, R1B[:, 1024:1536], R1B[:, 1536:2048])
                mm2r(PB[6][0:1, 0:512], SC_ONEC[:], R1B[:, 0:512], R1B[:, 512:1024],
                     True, True)
                mm2r(PB[7][0:1, 0:512], SC_ONEC[:], R1B[:, 1024:1536],
                     R1B[:, 1536:2048], True, True)
                s0, s1 = sl * 512, (sl + 1) * 512
                ts1("vector", ST_MU, PB[6][0:1, 0:512], 1.0 / 1024.0, AL.mult)
                ts1("vector", ST_TB, PB[7][0:1, 0:512], 1.0 / 1024.0, AL.mult)
                tt("vector", ST_TA, ST_MU, ST_MU, AL.mult)
                tt("vector", ST_TA, ST_TB, ST_TA, AL.subtract)
                act(ST_TB, ST_TA, AF.Sqrt, bias=1e-5)
                recip(ST_RS, ST_TB)
                stt("vector", ST_TA, ST_MU, -1.0, ST_RS, AL.mult, AL.mult)
                mm(PB[6][:, 0:512], ONEF64, ST_RS, True, True)
                mm(PB[7][:, 0:512], ONEF64, ST_TA, True, True)
                for c in range(8):
                    u = R1A[:, 5632:6144]
                    tt("vector", u, R1A[:, c * 512:(c + 1) * 512], PB[6][:, 0:512],
                       AL.mult)
                    tt("vector", u, u, PB[7][:, 0:512], AL.add)
                    split_sb(u, R2[:, c * 2048 + s0:c * 2048 + s1],
                             R3[:, c * 2048 + s0:c * 2048 + s1], lo_eng="gpsimd")
            if xn1h_tap is not None:
                b.dma("sync", xn1h_tap, R2[:, 0:16384])
                b.dma("sync", xn1l_tap, R3[:, 0:16384])

            _gate(8)
            b.dma("sync", R1A[:, 3072:3104], b1_d)
            b.dma("sync", R1A[:, 3104:3136], bqo_d)
            for cf in range(32):
                st = R1A[:, 0:1024]
                b.dma("sync", st, w1t_d[:, cf, :])
                split_sb(st, R1B[:, 0:1024], R1B[:, 1024:2048], lo_eng="gpsimd")
                st2 = R1A[:, 1024:1032]
                b.dma("sync", st2, wm_d[:, cf, :])
                split_sb(st2, R1B[:, 2048:2056], R1B[:, 2056:2064])
                split_sb(R1A[:, 3072 + cf:3073 + cf], R1B[:, 2064:2065],
                         R1B[:, 2065:2066])
                for half in range(2):
                    mm3(PB[half][0:8, 0:512],
                        R1B[:, 2048:2056], R1B[:, 2056:2064],
                        R1B[:, half * 512:half * 512 + 512],
                        R1B[:, 1024 + half * 512:1024 + half * 512 + 512],
                        start=(cf == 0), stop=(cf == 31))
                mm3(PB[2][0:8, 0:1], R1B[:, 2048:2056], R1B[:, 2056:2064],
                    R1B[:, 2064:2065], R1B[:, 2065:2066],
                    start=(cf == 0), stop=False)
                if cf == 31:
                    mm(PB[2][0:8, 0:1], RR["bm"], ONES512F[0:1, 0:1], False, True)
                st3 = R1A[:, 1032:1041]
                b.dma("sync", st3[:, 0:8], wqot_d[:, cf, :])
                cp("vector", st3[:, 8:9], R1A[:, 3104 + cf:3105 + cf])
                split_sb(st3, R1B[:, 2066:2075], R1B[:, 2075:2084])
                st4 = R1A[:, 2048:3072]
                b.dma("sync", st4, w2_d[:, cf, :])
                split_sb(st4, R1B[:, 2084:3108], R1B[:, 3108:4132], lo_eng="gpsimd")
                for half in range(2):
                    mm3(PB[4 + half][0:9, 0:512],
                        R1B[:, 2066:2075], R1B[:, 2075:2084],
                        R1B[:, 2084 + half * 512:2084 + half * 512 + 512],
                        R1B[:, 3108 + half * 512:3108 + half * 512 + 512],
                        start=(cf == 0), stop=(cf == 31))
            cp("vector", SC_TM[0:8, 3:4], PB[2][0:8, 0:1])
            cp("vector", R1A[0:8, 0:512], PB[0][0:8, 0:512])
            cp("vector", R1A[0:8, 512:1024], PB[1][0:8, 0:512])
            for dc in range(8):
                trans(PB[6][:, 0:8], R1A[0:8, dc * 128:dc * 128 + 128])
                cp("vector", R1A[:, 1024 + dc * 8:1024 + dc * 8 + 8], PB[6][:, 0:8])
            b.dma("scalar", wf1_sp, R1A[:, 1024:1088])
            cp("vector", R1A[0:9, 64:576], PB[4][0:9, 0:512])
            cp("vector", R1A[0:9, 576:1088], PB[5][0:9, 0:512])
            b.dma("scalar", wf2_sp[:, 0:512], R1A[0:9, 64:576])
            b.dma("scalar", wf2_sp[:, 512:1024], R1A[0:9, 576:1088])

            _gate(9)
            b.dma("sync", R1A[:, 0:64], wc2q_d.rearrange("p h q -> p (h q)"))
            split_sb(R1A[:, 0:64], R1B[:, 4096:4160], R1B[:, 4160:4224])
            for side in range(2):
                wsrc = wq1_d if side == 0 else wk1_d
                bcol = "bq1" if side == 0 else "bk1"
                for h in range(8):
                    st = R1A[:, 1024:2048]
                    b.dma("sync", st, wsrc[:, :, h, :])
                    split_sb(st, R1B[:, 0:1024], R1B[:, 1024:2048], lo_eng="gpsimd")
                    for sl in range(4):
                        s0, s1 = sl * 512, (sl + 1) * 512
                        p = PB[2][:, 0:512]
                        for c in range(8):
                            mm3(p, R1B[:, c * 128:c * 128 + 128],
                                R1B[:, 1024 + c * 128:1024 + c * 128 + 128],
                                R2[:, c * 2048 + s0:c * 2048 + s1],
                                R3[:, c * 2048 + s0:c * 2048 + s1],
                                start=(c == 0), stop=(c == 7))
                        if side == 0:
                            split_ps(p, R1B[:, 2048 + s0:2048 + s1],
                                     R1B[:, 4224 + s0:4224 + s1],
                                     bias_ap=bias(bcol, h))
                        else:
                            split_ps(p, R4[:, 2048 * h + s0:2048 * h + s1],
                                     R5[:, 2048 * h + s0:2048 * h + s1],
                                     bias_ap=bias(bcol, h))
                    if side == 0:
                        Qh = R1B[:, 2048:4096]
                        Ql = R1B[:, 4224:6272]
                        b.dma("scalar", q1h_sp[h], Qh)
                        b.dma("scalar", q1l_sp[h], Ql)
                    for tc in range(16):
                        if side == 0:
                            lh = Qh[:, tc * 128:(tc + 1) * 128]
                            ll = Ql[:, tc * 128:(tc + 1) * 128]
                        else:
                            lh = R4[:, 2048 * h + tc * 128:2048 * h + (tc) * 128 + 128]
                            ll = R5[:, 2048 * h + tc * 128:2048 * h + (tc) * 128 + 128]
                        pq = PB[3][:, tc * 8:tc * 8 + 8]
                        mm3(pq, lh, ll,
                            R1B[:, 4096 + h * 8:4096 + h * 8 + 8],
                            R1B[:, 4160 + h * 8:4160 + h * 8 + 8],
                            start=True, stop=False)
                        mm(pq, ONES512F[0:1, 0:128],
                           RR["bc2q"][0:1, h * 8:h * 8 + 8], False, True)
                    act(R1A[:, 0:128], PB[3][:, 0:128], AF.Square)
                    red(R1A[:, 128:144],
                        R1A[:, 0:128].rearrange("p (a c) -> p a c", c=8), AL.add)
                    act(R1A[:, 144:160], R1A[:, 128:144], AF.Sqrt)
                    recip(R1A[:, 160:176], R1A[:, 144:160])
                    for tc in range(16):
                        ts1("vector", R1A[:, 304 + tc * 8:304 + tc * 8 + 8],
                            PB[3][:, tc * 8:tc * 8 + 8],
                            R1A[:, 160 + tc:161 + tc], AL.mult)
                        trans(PB[0][0:8, 0:128], R1A[:, 304 + tc * 8:304 + tc * 8 + 8])
                        cp("vector", SC_TM2[0:8, 0:128], PB[0][0:8, 0:128])
                        b.dma("scalar",
                              (qq_sp if side == 0 else kq_sp)[h * 8:h * 8 + 8,
                                                              tc * 128:(tc + 1) * 128],
                              SC_TM2[0:8, 0:128])
            for half in range(2):
                for c in range(8):
                    st = R1A[:, 1024:1536]
                    b.dma("sync", st, wv1_d[:, c, half * 512:half * 512 + 512])
                    split_sb(st, R1B[:, 4224 + c * 512:4224 + c * 512 + 512],
                             R1B[:, 8320 + c * 512:8320 + c * 512 + 512],
                             lo_eng="gpsimd")
                for tgt in range(16):
                    p = PB[2][:, 0:512]
                    for c in range(8):
                        mm3(p, R2[:, c * 2048 + tgt * 128:c * 2048 + tgt * 128 + 128],
                            R3[:, c * 2048 + tgt * 128:c * 2048 + tgt * 128 + 128],
                            R1B[:, 4224 + c * 512:4224 + c * 512 + 512],
                            R1B[:, 8320 + c * 512:8320 + c * 512 + 512],
                            start=(c == 0), stop=False)
                    mm(p, SC_ONEF[0:1, 0:128],
                       SMR[0:1, 1024 + half * 512:1024 + half * 512 + 512],
                       False, True)
                    split_ps(p, R1B[:, 0:512], R1B[:, 512:1024])
                    b.dma("scalar", v1h_sp[tgt][:, half * 512:half * 512 + 512],
                          R1B[:, 0:512])
                    b.dma("scalar", v1l_sp[tgt][:, half * 512:half * 512 + 512],
                          R1B[:, 512:1024])

            _gate(10)
            SCALE1 = 0.5 / math.sqrt(128.0)
            for h in range(8):
                b.dma("sync", R1B[:, 0:2048], q1h_sp[h])
                b.dma("sync", R1B[:, 2048:4096], q1l_sp[h])
                b.dma("sync", R1A[0:8, 4096:6144], kq_sp[h * 8:h * 8 + 8, :])
                b.dma("sync", R1A[0:8, 6144:8192], qq_sp[h * 8:h * 8 + 8, :])
                for ip in range(2):
                    for j in range(16):
                        b.dma("sync", R1B[:, 4096 + (j % 2) * 1024:4096 + (j % 2) * 1024 + 1024],
                              v1h_sp[j])
                        b.dma("sync", R1B[:, 6144 + (j % 2) * 1024:6144 + (j % 2) * 1024 + 1024],
                              v1l_sp[j])
                        for ii in range(2):
                            i = ip * 2 + ii
                            mm(PB[0][:, 0:512],
                               R1A[0:8, 4096 + j * 128:4096 + j * 128 + 128],
                               R1A[0:8, 6144 + i * 512:6144 + i * 512 + 512],
                               True, True)
                            mm3(PB[1][:, 0:512],
                                R4[:, 2048 * h + j * 128:2048 * h + j * 128 + 128],
                                R5[:, 2048 * h + j * 128:2048 * h + j * 128 + 128],
                                R1B[:, i * 512:(i + 1) * 512],
                                R1B[:, 2048 + i * 512:2048 + i * 512 + 512],
                                start=True, stop=True)
                            fidc = R1A[:, 0:512]
                            ts2("vector", fidc, PB[0][:, 0:512], 1.0, -1.0,
                                AL.min, AL.max)
                            stile = R1A[:, 512:1024]
                            act(stile, fidc, AF.Sin, scale=PI)
                            u = R1A[:, 1024:1536]
                            tt("gpsimd", u, fidc, stile, AL.mult)
                            t1 = R1A[:, 1536:2048]
                            stt("vector", t1, u, 0.1, fidc, AL.mult, AL.add)
                            ein = R1A[:, 2048:2560]
                            stt("vector", ein, PB[1][:, 0:512], 2.0 * SCALE1, t1,
                                AL.mult, AL.add)
                            Pt = R1A[:, 2560:3072]
                            act(Pt, ein, AF.Exp, scale=0.5)
                            Phi = R1B[:, 8192:8704]
                            Plo = R1B[:, 8704:9216]
                            act(Phi, Pt, AF.Copy)
                            tt("gpsimd", Plo, Pt, Phi, AL.subtract)
                            mm2r(PB[6 + ii][0:1, 0:512], SC_ONEC[:], Phi, Plo,
                                 start=(j == 0), stop=(j == 15))
                            vt = R1B[:, 4096 + (j % 2) * 1024 + h * 128:
                                     4096 + (j % 2) * 1024 + h * 128 + 128]
                            vtl = R1B[:, 6144 + (j % 2) * 1024 + h * 128:
                                      6144 + (j % 2) * 1024 + h * 128 + 128]
                            pv = PB[2 + ii][:, 0:512]
                            with b.relax():
                                tw = b.wall()
                                mm(pv, vt, Phi, start=(j == 0), stop=False, waits=tw)
                                mm(pv, vt, Plo, start=False, stop=False)
                                mm(pv, vtl, Phi, start=False, stop=(j == 15))
                    for ii in range(2):
                        i = ip * 2 + ii
                        nrow = ST_MU
                        cp("vector", nrow, PB[6 + ii][0:1, 0:512])
                        rn = ST_RS
                        recip(rn, nrow)
                        mm(PB[4][:, 0:512], ONEF64, rn, True, True)
                        rnf = R1A[:, 3072:3584]
                        cp("vector", rnf, PB[4][:, 0:512])
                        pvn = R1A[:, 3584:4096]
                        tt("vector", pvn, PB[2 + ii][:, 0:512], rnf, AL.mult)
                        split_sb(pvn, R1B[:, 8192:8704], R1B[:, 8704:9216])
                        b.dma("scalar", pvh_sp[h][i], R1B[:, 8192:8704])
                        b.dma("scalar", pvl_sp[h][i], R1B[:, 8704:9216])
            _gate(91)
            for hh in range(8):
                st = R1A[:, 0:1024]
                b.dma("sync", st, wo1_d[:, hh, :])
                split_sb(st, R4[:, hh * 1024:hh * 1024 + 1024],
                         R5[:, hh * 1024:hh * 1024 + 1024], lo_eng="gpsimd")
            _gate(92)
            for i in range(4):
                for hh in range(8):
                    b.dma("sync", R1B[:, hh * 512:(hh + 1) * 512], pvh_sp[hh][i])
                    b.dma("sync", R1B[:, 8192 + hh * 512:8192 + (hh + 1) * 512],
                          pvl_sp[hh][i])
                s0, s1 = i * 512, (i + 1) * 512
                for dc in range(8):
                    p = PB[dc % 2][:, 0:512]
                    with b.relax():
                        tw = b.wall()
                        for hh in range(8):
                            mm3(p, R4[:, hh * 1024 + dc * 128:hh * 1024 + dc * 128 + 128],
                                R5[:, hh * 1024 + dc * 128:hh * 1024 + dc * 128 + 128],
                                R1B[:, hh * 512:(hh + 1) * 512],
                                R1B[:, 8192 + hh * 512:8192 + (hh + 1) * 512],
                                start=(hh == 0), stop=(hh == 7),
                                waits=tw if hh == 0 else ())
                    pre = R1A[:, dc * 512:dc * 512 + 512]
                    ts1("vector", pre, p, bias("bo1", dc), AL.add)
                    tt("vector", pre, pre, R2[:, dc * 2048 + s0:dc * 2048 + s1], AL.add)
                    tt("vector", pre, pre, R3[:, dc * 2048 + s0:dc * 2048 + s1], AL.add)
                _gate(93)
                acc = R1A[:, 4096:4608]
                acq = R1A[:, 4608:5120]
                sq = R1A[:, 5120:5632]
                cp("vector", acc, R1A[:, 0:512])
                act(acq, R1A[:, 0:512], AF.Square)
                for c in range(1, 8):
                    tt("vector", acc, acc, R1A[:, c * 512:(c + 1) * 512], AL.add)
                    act(sq, R1A[:, c * 512:(c + 1) * 512], AF.Square)
                    tt("vector", acq, acq, sq, AL.add)
                split_sb(acc, R1B[:, 9216:9728], R1B[:, 9728:10240])
                split_sb(acq, R1B[:, 10240:10752], R1B[:, 10752:11264])
                mm2r(PB[6][0:1, 0:512], SC_ONEC[:], R1B[:, 9216:9728],
                     R1B[:, 9728:10240], True, True)
                mm2r(PB[7][0:1, 0:512], SC_ONEC[:], R1B[:, 10240:10752],
                     R1B[:, 10752:11264], True, True)
                ts1("vector", ST_MU, PB[6][0:1, 0:512], 1.0 / 1024.0, AL.mult)
                ts1("vector", ST_TB, PB[7][0:1, 0:512], 1.0 / 1024.0, AL.mult)
                tt("vector", ST_TA, ST_MU, ST_MU, AL.mult)
                tt("vector", ST_TA, ST_TB, ST_TA, AL.subtract)
                act(ST_TB, ST_TA, AF.Sqrt, bias=1e-5)
                recip(ST_RS, ST_TB)
                stt("vector", ST_TA, ST_MU, -1.0, ST_RS, AL.mult, AL.mult)
                mm(PB[6][:, 0:512], ONEF64, ST_RS, True, True)
                mm(PB[7][:, 0:512], ONEF64, ST_TA, True, True)
                _gate(94)
                for dc in range(8):
                    u = R1A[:, 5632:6144]
                    tt("vector", u, R1A[:, dc * 512:dc * 512 + 512], PB[6][:, 0:512],
                       AL.mult)
                    tt("vector", u, u, PB[7][:, 0:512], AL.add)
                    ts2("vector", R1A[:, dc * 512:dc * 512 + 512], u,
                        bias("g1", dc), bias("be1", dc), AL.mult, AL.add)
                _gate(95)
                for dc in range(8):
                    if upto == 96:
                        mset(R1B[:, dc * 512:dc * 512 + 512], 0.0)
                        mset(R1B[:, 4096 + dc * 512:4096 + dc * 512 + 512], 0.0,
                             eng="gpsimd")
                    else:
                        split_sb(R1A[:, dc * 512:dc * 512 + 512],
                                 R1B[:, dc * 512:dc * 512 + 512],
                                 R1B[:, 4096 + dc * 512:4096 + dc * 512 + 512],
                                 lo_eng="gpsimd")
                _gate(96.5)
                _gate(98)
                b.dma("sync", SC_TM2[:, 0:64], wf1_sp)
                split_sb(SC_TM2[:, 0:64], R1B[:, 8192:8256], R1B[:, 8256:8320])
                pq = PB[2][0:8, 0:512]
                for dc in range(8):
                    mm3(pq, R1B[:, 8192 + dc * 8:8192 + dc * 8 + 8],
                        R1B[:, 8256 + dc * 8:8256 + dc * 8 + 8],
                        R1B[:, dc * 512:dc * 512 + 512],
                        R1B[:, 4096 + dc * 512:4096 + dc * 512 + 512],
                        start=(dc == 0), stop=(dc == 7))
                tanh = R1A[0:8, 4096:4608]
                act(tanh, pq, AF.Tanh, bias=SC_TM[0:8, 3:4])
                sarg = R1A[0:8, 4608:5120]
                ts2("vector", sarg, pq, 1.0, SC_TM[0:8, 3:4], AL.mult, AL.add)
                ts2("vector", sarg, sarg, PI / 8.0, PI, AL.mult, AL.min)
                ts1("vector", sarg, sarg, -PI, AL.max)
                ssin = R1A[0:8, 5120:5632]
                act(ssin, sarg, AF.Sin)
                tt("vector", sarg, ssin, ssin, AL.mult)
                ts2("vector", sarg, sarg, -2.0, 1.0, AL.mult, AL.add)
                qe = R1A[0:8, 5632:6144]
                tt("vector", qe, tanh, sarg, AL.mult)
                b.dma("sync", R1A[8:9, 5632:6144], ONES512F)
                split_sb(R1A[0:9, 5632:6144], R1B[0:9, 8320:8832],
                         R1B[0:9, 8832:9344])
                b.dma("sync", R1A[0:9, 4096:5120], wf2_sp)
                split_sb(R1A[0:9, 4096:5120], R1B[0:9, 9344:10368],
                         R1B[0:9, 10368:11392])
                for dc in range(8):
                    p = PB[3][:, 0:512]
                    mm3(p, R1B[0:9, 9344 + dc * 128:9344 + dc * 128 + 128],
                        R1B[0:9, 10368 + dc * 128:10368 + dc * 128 + 128],
                        R1B[0:9, 8320:8832], R1B[0:9, 8832:9344],
                        start=True, stop=True)
                    pre2 = R1A[:, 4096 + dc * 512:4096 + dc * 512 + 512]
                    ts1("vector", pre2, p, bias("b2e", dc), AL.add)
                    tt("vector", pre2, pre2, R1A[:, dc * 512:dc * 512 + 512], AL.add)
                acc = R1A[:, 0:512]
                acq = R1A[:, 512:1024]
                sq = R1A[:, 1024:1536]
                cp("vector", acc, R1A[:, 4096:4608])
                act(acq, R1A[:, 4096:4608], AF.Square)
                for c in range(1, 8):
                    tt("vector", acc, acc, R1A[:, 4096 + c * 512:4096 + (c + 1) * 512],
                       AL.add)
                    act(sq, R1A[:, 4096 + c * 512:4096 + (c + 1) * 512], AF.Square)
                    tt("vector", acq, acq, sq, AL.add)
                split_sb(acc, R1B[:, 9216:9728], R1B[:, 9728:10240])
                split_sb(acq, R1B[:, 10240:10752], R1B[:, 10752:11264])
                mm2r(PB[6][0:1, 0:512], SC_ONEC[:], R1B[:, 9216:9728],
                     R1B[:, 9728:10240], True, True)
                mm2r(PB[7][0:1, 0:512], SC_ONEC[:], R1B[:, 10240:10752],
                     R1B[:, 10752:11264], True, True)
                ts1("vector", ST_MU, PB[6][0:1, 0:512], 1.0 / 1024.0, AL.mult)
                ts1("vector", ST_TB, PB[7][0:1, 0:512], 1.0 / 1024.0, AL.mult)
                tt("vector", ST_TA, ST_MU, ST_MU, AL.mult)
                tt("vector", ST_TA, ST_TB, ST_TA, AL.subtract)
                act(ST_TB, ST_TA, AF.Sqrt, bias=1e-5)
                recip(ST_RS, ST_TB)
                stt("vector", ST_TA, ST_MU, -1.0, ST_RS, AL.mult, AL.mult)
                mm(PB[6][:, 0:512], ONEF64, ST_RS, True, True)
                mm(PB[7][:, 0:512], ONEF64, ST_TA, True, True)
                for dc in range(8):
                    u = R1A[:, 1536:2048]
                    tt("vector", u, R1A[:, 4096 + dc * 512:4096 + dc * 512 + 512],
                       PB[6][:, 0:512], AL.mult)
                    tt("vector", u, u, PB[7][:, 0:512], AL.add)
                    ts2("vector", R1A[:, 2048:2560], u, bias("g2", dc),
                        bias("be2", dc), AL.mult, AL.add)
                    b.dma("scalar", s_sp[i][:, dc * 512:dc * 512 + 512],
                          R1A[:, 2048:2560])

            _gate(100)
            CAP = 1280
            SLOTS = 8 * CAP
            mset(R1A[:, 0:4096], 0.0)
            for qq4 in range(4):
                b.dma("sync", g_tok[:, qq4 * 512:(qq4 + 1) * 512, :]
                      .rearrange("p t g -> p (t g)"), R1A[:, 0:4096])
            mset(R1B[:, 0:128], 1.0)
            b.dma("sync", R1A[:, 0:64], rw_d.rearrange("p c e -> p (c e)"))
            split_sb(R1A[:, 0:64], R1B[:, 128:192], R1B[:, 192:256])
            b.dma("sync", R1B[:, 384:512], tri128_d)
            b.dma("sync", R1A[:, 0:512], bcfree_d)
            for tc in range(16):
                pl = PB[0][:, tc * 8:tc * 8 + 8]
                for c in range(8):
                    mm3(pl, R2[:, c * 2048 + tc * 128:c * 2048 + tc * 128 + 128],
                        R3[:, c * 2048 + tc * 128:c * 2048 + tc * 128 + 128],
                        R1B[:, 128 + c * 8:128 + c * 8 + 8],
                        R1B[:, 192 + c * 8:192 + c * 8 + 8],
                        start=(c == 0), stop=False)
                mm(pl, ONES512F[0:1, 0:128], RR["rb"], False, True)
                cp("vector", R1A[:, 600:608], pl)
                b.dma("scalar", dram["lg_sp"][:, tc * 8:tc * 8 + 8], R1A[:, 600:608])
                m = R1A[:, 512:513]
                red(m, pl, AL.max)
                nm = R1A[:, 513:514]
                red(nm, pl, AL.max, negate=True)
                ex = R1A[:, 514:522]
                act(ex, pl, AF.Exp, bias=nm)
                sm = R1A[:, 522:523]
                red(sm, ex, AL.add)
                recip(SC_TM[:, 16 + tc:17 + tc], sm)
                eq = R1A[:, 523:531]
                ts1("vector", eq, pl, m, AL.is_ge)
                t9 = R1A[:, 531:539]
                tt("vector", t9, eq, SC_B[:, BOFF["c999"]:BOFF["c999"] + 8], AL.mult)
                v9 = R1A[:, 539:540]
                red(v9, t9, AL.max)
                ts2("vector", SC_TM[:, 32 + tc:33 + tc], v9, -1.0, 999.0,
                    AL.mult, AL.add)
                ts1("vector", R1A[:, 540:548],
                    SC_B[:, BOFF["eidx"]:BOFF["eidx"] + 8],
                    SC_TM[:, 32 + tc:33 + tc], AL.is_equal)
                cp("vector", R1B[:, 256 + tc * 8:256 + tc * 8 + 8], R1A[:, 540:548])
            for tc in range(16):
                pr = PB[1][:, tc * 8:tc * 8 + 8]
                if tc == 0:
                    mm(pr, R1B[:, 384:512], R1B[:, 256:264], True, True)
                else:
                    for tcp in range(tc):
                        mm(pr, R1B[:, 0:128], R1B[:, 256 + tcp * 8:256 + tcp * 8 + 8],
                           start=(tcp == 0), stop=False)
                    mm(pr, R1B[:, 384:512], R1B[:, 256 + tc * 8:256 + tc * 8 + 8],
                       False, True)
                t9 = R1A[:, 548:556]
                tt("vector", t9, pr, R1B[:, 256 + tc * 8:256 + tc * 8 + 8], AL.mult)
                red(SC_TM[:, 48 + tc:49 + tc], t9, AL.add)
                vm = R1A[:, 556:557]
                ts1("vector", vm, SC_TM[:, 48 + tc:49 + tc], float(CAP) - 0.5, AL.is_lt)
                bcv = R1A[:, 557:558]
                stt("vector", bcv, SC_TM[:, 32 + tc:33 + tc], float(CAP),
                    SC_TM[:, 48 + tc:49 + tc], AL.mult, AL.add)
                tt("vector", bcv, bcv, vm, AL.mult)
                u9 = R1A[:, 558:559]
                ts2("vector", u9, vm, -float(SLOTS), float(SLOTS), AL.mult, AL.add)
                tt("vector", SC_TM[:, 64 + tc:65 + tc], bcv, u9, AL.add)
            trans(PB[2][0:16, 0:128], SC_TM[:, 16:32])
            cp("vector", SC_TM2[0:16, :], PB[2][0:16, 0:128])
            b.dma("gpsimd", bass.AP(dram["mp_ln"].tensor, 0, [[128, 16], [1, 128]]),
                  SC_TM2[0:16, :])
            _gate(101)
            b.dma("sync", R1B[:, 512:560], invvals_d.rearrange("p t c -> p (t c)"))
            NB = SLOTS // 128
            NG = SLOTS // 512
            for g in range(NG):
                for tc in range(16):
                    Ptile = R1B[:, 1024:1536]
                    ts2("vector", Ptile, R1A[:, 0:512], float(512 * g),
                        SC_TM[:, 64 + tc:65 + tc], AL.add, AL.is_equal)
                    for bq in range(4):
                        mm(PB[3 + bq][:, g * 3:g * 3 + 3],
                           Ptile[:, bq * 128:bq * 128 + 128],
                           R1B[:, 512 + tc * 3:512 + tc * 3 + 3],
                           start=(tc == 0), stop=(tc == 15))
            for bq in range(4):
                cp("vector",
                   R1A[:, 512:512 + 3 * NB].rearrange("p (g b c) -> p g b c",
                                                      b=4, c=3)[:, :, bq, :],
                   PB[3 + bq][:, 0:3 * NG])
            hiv = R1A[:, 512:512 + 3 * NB].rearrange("p (b c) -> p b c", c=3)
            stt("vector", R1A[:, 752:752 + NB], hiv[:, :, 0], 128.0, hiv[:, :, 1],
                AL.mult, AL.add)
            ts2("vector", R1A[:, 832:832 + NB], R1A[:, 752:752 + NB], -1.0, 0.0,
                AL.add, AL.max)
            ts1("vector", R1A[:, 912:912 + NB], R1A[:, 832:832 + NB], 8.0, AL.mult)
            trans(PB[2][0:NB, 0:128], R1A[:, 912:912 + NB])
            cp("vector", SC_TM2[0:NB, :], PB[2][0:NB, 0:128])
            b.dma("gpsimd",
                  bass.AP(dram["grid_sp"].tensor, 0, [[1024, NB], [128, 8], [1, 16]]),
                  SC_TM2[0:NB, :].rearrange("b (h l) -> b h l", l=16))
            b.dma("sync", SC_IDX[:, 0:SLOTS // 16], dram["grid_sp"], transpose=True)
            b.dma("sync", SC_IDX[16:32, :], SC_IDX[0:16, :])
            b.dma("sync", SC_IDX[32:64, :], SC_IDX[0:32, :])
            b.dma("sync", SC_IDX[64:128, :], SC_IDX[0:64, :])
            _gate(102)
            for c in range(8):
                cp("vector", R4[:, 0:16384].rearrange("p (t g) -> p t g", g=8)[:, :, c],
                   R2[:, c * 2048:(c + 1) * 2048])
                cp("gpsimd", R5[:, 0:16384].rearrange("p (t g) -> p t g", g=8)[:, :, c],
                   R3[:, c * 2048:(c + 1) * 2048])
            for bi in range(40):
                e, kb = bi // 5, bi % 5
                icol = (CAP // 16) * e + 16 * kb
                for hf in range(2):
                    idx = SC_IDX[:, icol + hf * 8:icol + hf * 8 + 8]
                    b.emit("gpsimd",
                           lambda ee, O=R1B[:, hf * 1024:hf * 1024 + 1024]
                           .rearrange("p (i g) -> p i g", g=8),
                           D=R4[:, 0:16384].rearrange("p (t g) -> p t g", g=8),
                           I=idx: ee.indirect_copy(O, D, I, True),
                           waits=b.wall(), inc="G")
                    b.emit("gpsimd",
                           lambda ee, O=R1B[:, 2048 + hf * 1024:2048 + hf * 1024 + 1024]
                           .rearrange("p (i g) -> p i g", g=8),
                           D=R5[:, 0:16384].rearrange("p (t g) -> p t g", g=8),
                           I=idx: ee.indirect_copy(O, D, I, True),
                           waits=b.wall(), inc="G")
                for c in range(8):
                    cp("vector", R1B[:, 4096 + c * 256:4096 + c * 256 + 256],
                       R1B[:, 0:2048].rearrange("p (i g) -> p i g", g=8)[:, :, c])
                    cp("gpsimd", R1B[:, 6144 + c * 256:6144 + c * 256 + 256],
                       R1B[:, 2048:4096].rearrange("p (i g) -> p i g", g=8)[:, :, c])
                b.dma("scalar", buckh_sp[bi], R1B[:, 4096:6144])
                b.dma("scalar", buckl_sp[bi], R1B[:, 6144:8192])
            _gate(103)
            for e in range(8):
                _gate(120 + e)
                for (kbase, nkb) in ((0, 3), (3, 2)):
                    ycols = nkb * 2048
                    mset(R1A[:, ycols:ycols + 8], 0.0)
                    for kk in range(nkb):
                        bi = e * 5 + kbase + kk
                        b.dma("sync", R4[:, 0:2048], buckh_sp[bi])
                        b.dma("sync", R5[:, 0:2048], buckl_sp[bi])
                        for jq in range(8):
                            with b.relax():
                                mk = {}
                                for pc in range(16):
                                    p8, h2 = pc // 2, pc % 2
                                    buf = R1A[:, 6656 + h2 * 512:
                                              6656 + h2 * 512 + 512]
                                    d = b.dma("sync", buf,
                                              w1m_d[e][:, p8,
                                                       4 * jq + 2 * h2:
                                                       4 * jq + 2 * h2 + 2, :]
                                              .rearrange("p j u -> p (j u)"),
                                              waits=mk.get(pc - 2, ()))
                                    o = 2048 + p8 * 1024 + h2 * 512
                                    act(R4[:, o:o + 512], buf, AF.Copy,
                                        waits=(("dS", d),))
                                    sA = b.snap("S")
                                    tt("gpsimd", R5[:, o:o + 512], buf,
                                       R4[:, o:o + 512], AL.subtract,
                                       waits=(("dS", d), ("S", sA)))
                                    mk[pc] = (("S", sA), ("G", b.snap("G")))
                            for j4 in range(4):
                                jg = jq * 4 + j4
                                pa = PB[4][:, 0:256]
                                pg = PB[5][:, 0:256]
                                with b.relax():
                                    tw = b.wall()
                                    for c in range(8):
                                        o = 2048 + c * 1024 + j4 * 256
                                        mm3(pa, R4[:, o:o + 128], R5[:, o:o + 128],
                                            R4[:, c * 256:c * 256 + 256],
                                            R5[:, c * 256:c * 256 + 256],
                                            start=(c == 0), stop=(c == 7),
                                            waits=tw if c == 0 else ())
                                        mm3(pg, R4[:, o + 128:o + 256],
                                            R5[:, o + 128:o + 256],
                                            R4[:, c * 256:c * 256 + 256],
                                            R5[:, c * 256:c * 256 + 256],
                                            start=(c == 0), stop=(c == 7))
                                sa = R1A[:, 7680:7936]
                                act(sa, pa, AF.Silu)
                                tf = R1A[:, 7936:8192]
                                tt("vector", tf, sa, pg, AL.mult)
                                split_sb(tf, R1B[:, jg * 256:jg * 256 + 256],
                                         R1B[:, 8192 + jg * 256:8192 + jg * 256 + 256],
                                         lo_eng="gpsimd")
                        for cf4 in range(8):
                            with b.relax():
                                mk = {}
                                for pc in range(8):
                                    p4, h2 = pc // 2, pc % 2
                                    buf = R1A[:, 6656 + h2 * 512:
                                              6656 + h2 * 512 + 512]
                                    d = b.dma("sync", buf,
                                              w2m_d[e][:, cf4 * 4 + p4,
                                                       h2 * 512:h2 * 512 + 512],
                                              waits=mk.get(pc - 2, ()))
                                    o = 2048 + p4 * 1024 + h2 * 512
                                    act(R4[:, o:o + 512], buf, AF.Copy,
                                        waits=(("dS", d),))
                                    sA = b.snap("S")
                                    tt("gpsimd", R5[:, o:o + 512], buf,
                                       R4[:, o:o + 512], AL.subtract,
                                       waits=(("dS", d), ("S", sA)))
                                    mk[pc] = (("S", sA), ("G", b.snap("G")))
                            for p4 in range(4):
                                cf = cf4 * 4 + p4
                                with b.relax():
                                    tw = b.wall()
                                    for dc in range(8):
                                        ypb = PB[dc][:, 0:256]
                                        mm3(ypb,
                                            R4[:, 2048 + p4 * 1024 + dc * 128:2048 + p4 * 1024 + dc * 128 + 128],
                                            R5[:, 2048 + p4 * 1024 + dc * 128:2048 + p4 * 1024 + dc * 128 + 128],
                                            R1B[:, cf * 256:cf * 256 + 256],
                                            R1B[:, 8192 + cf * 256:8192 + cf * 256 + 256],
                                            start=(cf == 0), stop=(cf == 31),
                                            waits=tw if dc == 0 else ())
                        for dc in range(8):
                            ypb = PB[dc][:, 0:256]
                            ovw = R1A[:, kk * 2048:kk * 2048 + 2048] \
                                .rearrange("p (s g) -> p s g", g=8)[:, :, dc]
                            cp("vector", ovw, ypb)
                    _gate(150)
                    base = 256 * kbase
                    n = 256 * nkb
                    for tc in range(16):
                        e1 = R1A[:, 8184:8185]
                        ts1("vector", e1, SC_TM[:, 32 + tc:33 + tc], float(e),
                            AL.is_equal)
                        if base > 0:
                            e2 = R1A[:, 8185:8186]
                            ts1("vector", e2, SC_TM[:, 48 + tc:49 + tc],
                                float(base) - 0.5, AL.is_ge)
                            tt("vector", e1, e1, e2, AL.mult)
                        e3 = R1A[:, 8186:8187]
                        ts1("vector", e3, SC_TM[:, 48 + tc:49 + tc],
                            float(base + n) - 0.5, AL.is_lt)
                        tt("vector", e1, e1, e3, AL.mult)
                        rl8 = R1A[:, 8187:8188]
                        ts2("vector", rl8, SC_TM[:, 48 + tc:49 + tc], -float(base),
                            8.0, AL.add, AL.mult)
                        tt("vector", rl8, rl8, e1, AL.mult)
                        u0 = R1A[:, 8188:8189]
                        ts2("vector", u0, e1, -float(ycols), float(ycols),
                            AL.mult, AL.add)
                        tt("vector", R1A[:, 8160 + tc:8161 + tc], rl8, u0, AL.add)
                    trans(PB[5][0:16, 0:128], R1A[:, 8160:8176])
                    cp("vector", SC_TM2[0:16, :], PB[5][0:16, 0:128])
                    b.dma("gpsimd",
                          bass.AP(dram["grid_sp"].tensor, 0,
                                  [[1024, 16], [128, 8], [1, 16]]),
                          SC_TM2[0:16, :].rearrange("b (h l) -> b h l", l=16))
                    b.dma("sync", SC_IDX[:, 0:128], dram["grid_sp"][0:128, :],
                          transpose=True)
                    b.dma("sync", SC_IDX[16:32, 0:128], SC_IDX[0:16, 0:128])
                    b.dma("sync", SC_IDX[32:64, 0:128], SC_IDX[0:32, 0:128])
                    b.dma("sync", SC_IDX[64:128, 0:128], SC_IDX[0:64, 0:128])
                    for sg in range(7):
                        w = 1024 if sg < 6 else 8
                        split_sb(R1A[:, sg * 1024:sg * 1024 + w],
                                 R2[:, sg * 1024:sg * 1024 + w],
                                 R3[:, sg * 1024:sg * 1024 + w],
                                 lo_eng="gpsimd")
                    _gate(151)
                    for q8 in range(16):
                        if q8 == 1:
                            _gate(152)
                        idx = SC_IDX[:, q8 * 8:q8 * 8 + 8]
                        b.emit("gpsimd",
                               lambda ee, O=R1B[:, 0:1024]
                               .rearrange("p (i g) -> p i g", g=8),
                               D=R2[:, 0:16384].rearrange("p (t g) -> p t g", g=8),
                               I=idx: ee.indirect_copy(O, D, I, True),
                               waits=b.wall(), inc="G")
                        b.emit("gpsimd",
                               lambda ee, O=R1B[:, 1024:2048]
                               .rearrange("p (i g) -> p i g", g=8),
                               D=R3[:, 0:16384].rearrange("p (t g) -> p t g", g=8),
                               I=idx: ee.indirect_copy(O, D, I, True),
                               waits=b.wall(), inc="G")
                        tt("vector", R1A[:, 6656:7680], R1B[:, 0:1024],
                           R1B[:, 1024:2048], AL.add)
                        gslc = g_tok[:, q8 * 128:q8 * 128 + 128, :] \
                            .rearrange("p t g -> p (t g)")
                        b.dma("sync", R1A[:, 7680:8192], gslc[:, 0:512])
                        tt("vector", R1A[:, 7680:8192], R1A[:, 7680:8192],
                           R1A[:, 6656:7168], AL.add)
                        b.dma("scalar", gslc[:, 0:512], R1A[:, 7680:8192])
                        b.dma("sync", R1A[:, 7680:8192], gslc[:, 512:1024])
                        tt("vector", R1A[:, 7680:8192], R1A[:, 7680:8192],
                           R1A[:, 7168:7680], AL.add)
                        b.dma("scalar", gslc[:, 512:1024], R1A[:, 7680:8192])

            _gate(200)
            b.dma("sync", SMR[64:65, 0:2048],
                  dram["mp_ln"].rearrange("(a b) -> a b", a=1))
            for sl in range(4):
                b.dma("sync", R1A[:, 0:4096],
                      g_tok[:, sl * 512:(sl + 1) * 512, :].rearrange("p t g -> p (t g)"))
                mm(PB[6][:, 0:512], ONEF64,
                   SMR[64:65, sl * 512:sl * 512 + 512], True, True)
                mpb = R1A[:, 5632:6144]
                cp("vector", mpb, PB[6][:, 0:512])
                for dc in range(8):
                    xpc = R1A[:, 4096:4608]
                    spc = R1A[:, 4608:5120]
                    b.dma("sync", xpc, x1_sp[sl][:, dc * 512:dc * 512 + 512])
                    b.dma("sync", spc, s_sp[sl][:, dc * 512:dc * 512 + 512])
                    gview = R1A[:, 0:4096].rearrange("p (t g) -> p t g", g=8)[:, :, dc]
                    tg = R1A[:, 6144:6656]
                    tt("vector", tg, gview, mpb, AL.mult)
                    sg = R1A[:, 6656:7168]
                    act(sg, tg, AF.Silu)
                    t2 = R1A[:, 7168:7680]
                    tt("vector", t2, sg, spc, AL.mult)
                    ov = R1A[:, 7680:8192]
                    tt("vector", ov, t2, xpc, AL.add)
                    b.dma("scalar", out_fm[dc][:, sl * 512:sl * 512 + 512], ov)

        except _Done:
            pass

        fw = b.wall()
        for eng in ENGS:
            b.emit(eng, None, waits=fw)

        blk = en(nc.Block())
        b.replay(blk)
    return nc, dram


def _bf16(x):
    import ml_dtypes
    return np.asarray(x, dtype=ml_dtypes.bfloat16)


def _f32(x):
    return np.ascontiguousarray(np.asarray(x, dtype=np.float32))


def prep_inputs(I):
    d = {}
    d["wq_d"] = _f32(I["L0_attn_wq"].reshape(8, 128, 256).transpose(1, 0, 2))
    d["wqc_d"] = _f32(I["L0_attn_wqc"].reshape(2, 128, 896).transpose(1, 0, 2))
    wqr = I["L0_attn_wqr"]
    d["wqr_d"] = _f32(wqr.reshape(8, 128, 128).transpose(1, 0, 2))
    wqr_r = np.concatenate([wqr[:, 64:], wqr[:, :64]], axis=1)
    d["wqr_r"] = _f32(wqr_r.reshape(8, 128, 128).transpose(1, 0, 2))
    d["wkv_d"] = _f32(I["L0_attn_wkv"].reshape(8, 128, 256).transpose(1, 0, 2))
    wkr = I["L0_attn_wkr"]
    d["wkr_d"] = _f32(wkr.reshape(8, 128, 128).transpose(1, 0, 2))
    wkr_r = np.concatenate([wkr[:, 64:], wkr[:, :64]], axis=1)
    d["wkr_r"] = _f32(wkr_r.reshape(8, 128, 128).transpose(1, 0, 2))
    d["wkc_d"] = _f32(I["L0_attn_wkc"].reshape(2, 128, 896).transpose(1, 0, 2))
    d["wv_d"] = _f32(I["L0_attn_wv"].reshape(2, 128, 1024).transpose(1, 0, 2))
    d["wo_d"] = _f32(I["L0_attn_wo"].reshape(8, 128, 1024).transpose(1, 0, 2))
    d["k0_d"] = _f32(I["L0_ffn_kernel"].reshape(8, 128, 1024).transpose(1, 0, 2))
    d["a_d"] = _f32(I["L0_ffn_a"].reshape(8, 128, 16).transpose(1, 0, 2)
                    .reshape(128, 128))
    d["b_mat"] = _f32(I["L0_ffn_b"])
    d["wq1_d"] = _f32(I["L1_attn_Wq"].transpose(1, 0, 2).reshape(8, 128, 8, 128)
                      .transpose(1, 0, 2, 3))
    d["wk1_d"] = _f32(I["L1_attn_Wk"].transpose(1, 0, 2).reshape(8, 128, 8, 128)
                      .transpose(1, 0, 2, 3))
    d["wv1_d"] = _f32(I["L1_attn_Wv"].transpose(1, 0, 2).reshape(1024, 1024)
                      .reshape(8, 128, 1024).transpose(1, 0, 2))
    d["wc2q_d"] = _f32(I["L1_attn_Wc2q"].transpose(1, 0, 2))
    d["wo1_d"] = _f32(I["L1_attn_Wo"].reshape(8, 128, 1024).transpose(1, 0, 2))
    d["w1t_d"] = _f32(I["L1_attn_W1"].T.reshape(32, 128, 1024).transpose(1, 0, 2))
    d["wm_d"] = _f32(I["L1_attn_Wm"].reshape(32, 128, 8).transpose(1, 0, 2))
    d["wqot_d"] = _f32(I["L1_attn_Wqo"].T.reshape(32, 128, 8).transpose(1, 0, 2))
    d["bqo_d"] = _f32(I["L1_attn_bqo"].reshape(32, 128).T)
    d["b1_d"] = _f32(I["L1_attn_b1"].reshape(32, 128).T)
    d["w2_d"] = _f32(I["L1_attn_W2"].reshape(32, 128, 1024).transpose(1, 0, 2))
    d["rw_d"] = _f32(I["L1_ffn_router_w"].reshape(8, 128, 8).transpose(1, 0, 2))
    d["w1m_d"] = _f32(I["L1_ffn_w1"].reshape(8, 8, 128, 2, 32, 128)
                      .transpose(0, 2, 1, 4, 3, 5).reshape(8, 128, 8, 32, 256))
    d["w2m_d"] = _f32(I["L1_ffn_w2"].reshape(8, 32, 128, 1024).transpose(0, 2, 1, 3))

    bia = np.zeros((128, 120), dtype=np.float32)
    col = 0

    def put(arr, w):
        nonlocal col
        bia[:, col:col + w] = np.asarray(arr, dtype=np.float32).reshape(w, 128).T
        col += w

    put(I["L0_attn_bq"], 2)
    put(I["L0_attn_bqc"], 7)
    put(I["L0_attn_bkv"], 2)
    put(I["L0_attn_bkc"], 7)
    put(I["L0_attn_bo"], 8)
    put(I["L0_ffn_bias"], 8)
    put(I["L1_attn_bq"].reshape(8, 128), 8)
    put(I["L1_attn_bk"].reshape(8, 128), 8)
    put(I["L1_attn_bo"], 8)
    put(I["L1_attn_g1"], 8)
    put(I["L1_attn_beta1"], 8)
    put(I["L1_attn_g2"], 8)
    put(I["L1_attn_beta2"], 8)
    bia[:, col:col + 8] = np.arange(8, dtype=np.float32)[None, :]
    col += 8
    bia[:, col:col + 8] = 999.0 - np.arange(8, dtype=np.float32)[None, :]
    col += 8
    put(I["L1_attn_b2"], 8)
    d["biases_d"] = bia

    rows = np.zeros((8, 1024), dtype=np.float32)
    rows[0] = np.asarray(I["L0_attn_bv"], dtype=np.float32)
    rows[1] = np.asarray(I["L1_attn_bv"], dtype=np.float32).reshape(1024)
    bqr = np.asarray(I["L0_attn_bqr"], dtype=np.float32)
    bkr = np.asarray(I["L0_attn_bkr"], dtype=np.float32)
    rows[2, 0:128] = bqr
    rows[2, 128:256] = np.concatenate([bqr[64:], bqr[:64]])
    rows[2, 256:384] = bkr
    rows[2, 384:512] = np.concatenate([bkr[64:], bkr[:64]])
    rows[2, 512:576] = np.asarray(I["L1_attn_bc2q"], dtype=np.float32).reshape(64)
    rows[2, 576:584] = np.asarray(I["L1_attn_bm"], dtype=np.float32)
    rows[2, 584:592] = np.asarray(I["L1_ffn_router_b"], dtype=np.float32)
    d["rows_d"] = rows

    d["identf"] = np.eye(128, dtype=np.float32)
    inv_freq = (1.0 / 10000.0 ** (np.arange(0, 128, 2, dtype=np.float64) / 128.0))
    t = np.arange(2048, dtype=np.float64)
    fr = t[None, :] * inv_freq[:, None]
    cs = np.zeros((128, 4096), dtype=np.float32)
    cs[0:64, 0:2048] = np.cos(fr)
    cs[64:128, 0:2048] = np.cos(fr)
    cs[0:64, 2048:4096] = -np.sin(fr)
    cs[64:128, 2048:4096] = np.sin(fr)
    d["cossin_d"] = cs
    kp = np.arange(128)[:, None, None]
    dd = np.arange(4)[None, :, None]
    qf = np.arange(512)[None, None, :]
    mask = np.where(qf >= 128 * dd + kp, 0.0, -1.0e9).astype(np.float32)
    d["masks_d"] = _bf16(mask)
    iv = np.zeros((128, 16, 3), dtype=np.float32)
    tt = 128 * np.arange(16)[None, :] + np.arange(128)[:, None] + 1
    iv[:, :, 0] = tt // 128
    iv[:, :, 1] = tt % 128
    iv[:, :, 2] = 1.0
    d["invvals_d"] = _bf16(iv)
    d["bcfree_d"] = np.broadcast_to(
        np.arange(512, dtype=np.float32)[None, :], (128, 512)).copy()
    tri = (np.arange(128)[:, None] < np.arange(128)[None, :]).astype(np.float32)
    d["tri128_d"] = _bf16(tri)
    return d


def kernel(**inputs):
    nc, _ = build_program()
    shared = prep_inputs(inputs)
    x = np.asarray(inputs["L0_attn_x"], dtype=np.float32)
    in_maps = []
    for c in range(8):
        m = dict(shared)
        m["x_fm"] = np.ascontiguousarray(
            x[c].T.reshape(8, 128, 2048))
        in_maps.append(m)
    res = run_bass_kernel_spmd(nc, in_maps, core_ids=list(range(8)))
    outs = []
    for c in range(8):
        of = np.asarray(res.results[c]["out_fm"], dtype=np.float32)
        outs.append(of.reshape(1024, 2048).T)
    return np.ascontiguousarray(np.stack(outs, axis=0))
